# version 08
# speedup vs baseline: 1.3427x; 1.0195x over previous
import contextlib
import numpy as np
import ml_dtypes
import concourse.bass as bass
import concourse.mybir as mybir
from concourse.bass_utils import run_bass_kernel_spmd

F32 = mybir.dt.float32
BF16 = mybir.dt.bfloat16
F16 = mybir.dt.float16
AF = mybir.ActivationFunctionType
ALU = mybir.AluOpType
AX = mybir.AxisListType
BFNP = ml_dtypes.bfloat16

NCORES = 8
BPC = 2
T = 2048
TOK = BPC * T
D = 1024
E = 2048
II = 2816
HCH = D // 128
ECH = E // 128
ICH = II // 128
NTT = TOK // 512
NJ = T // 128
EPS = 1e-5
CLIP_LO = -128.49
CLIP_HI = 127.49
MAGIC = 1536.0
QMAX = 127.0

ENGINES = ("sync", "gpsimd", "scalar", "vector", "tensor")


class Item:
    __slots__ = ("fn", "waits", "inc")

    def __init__(self, fn, waits, inc):
        self.fn = fn
        self.waits = waits
        self.inc = inc


class Sched:

    def __init__(self, nc, sems):
        self.nc = nc
        self.sems = sems
        self.items = {e: [] for e in ENGINES}
        self.val = {n: 0 for n in sems}
        self.vq_base = 0

    def emit(self, eng, fn, waits=(), inc=None, inc_n=1):
        tok = None
        if inc is not None:
            self.val[inc] += inc_n
            tok = (inc, self.val[inc], inc_n)
        self.items[eng].append(Item(fn, [w for w in waits if w is not None], tok))
        return tok

    def dma(self, eng, out, in_, waits=(), inc="dma", **kw):
        def fn(e):
            return e.dma_start(out=out, in_=in_, **kw)
        return self.emit(eng, fn, waits=waits, inc=inc, inc_n=16)

    def tok(self, sem):
        return (sem, self.val[sem], 0) if self.val[sem] else None

    def replay(self, blk):
        for eng in ENGINES:
            items = self.items[eng]
            if not items:
                continue

            def make(eng_items, ename):
                def body(e):
                    for it in eng_items:
                        for (sem, v, _s) in it.waits:
                            e.wait_ge(self.sems[sem], v)
                        inst = it.fn(e)
                        if it.inc is not None:
                            if inst is None:
                                inst = e.nop()
                            inst.then_inc(self.sems[it.inc[0]], it.inc[2])
                return body
            getattr(blk, eng)(make(items, eng))


def _sems(ctx, nc, names):
    return {n: ctx.enter_context(nc.semaphore(n)) for n in names}


SEMN = ["dct", "dwi", "xl0", "xl1", "wm0", "wm1", "dbb",
        "zst0", "zst1", "pe", "ve", "ae", "st", "vq"]


def build_l1():
    nc = bass.Bass("TRN2", target_bir_lowering=False, debug=False,
                   detect_race_conditions=False, dynamic_dma_scratch_size=8192)
    xrev = nc.dram_tensor("xrev", [NTT * 128, HCH * 512], F32,
                          kind="ExternalInput")
    cts = nc.dram_tensor("cts", [D, BPC], F32, kind="ExternalInput")
    wmod = nc.dram_tensor("wmod", [D, 6 * D], F32, kind="ExternalInput")
    w_inT = nc.dram_tensor("w_inT", [D, E], F32, kind="ExternalInput")
    b_in = nc.dram_tensor("b_in", [1, E], F32, kind="ExternalInput")
    zout = nc.dram_tensor("zout", [TOK, E], BF16, kind="ExternalOutput")
    mod_out = nc.dram_tensor("mod_out", [128, 96], F32, kind="ExternalOutput")
    zcd = nc.dram_tensor("zcd1", [1, 2 * E], F32, kind="Internal")

    ctx = contextlib.ExitStack()
    with ctx:
        sb = nc.sbuf_tensor
        wmod_db = ctx.enter_context(sb("wmod_db", [128, 2 * 6 * D], F32))
        ct_sb = ctx.enter_context(sb("ct_sb", [128, HCH * BPC], F32))
        silu_c = ctx.enter_context(sb("silu_c", [128, HCH * BPC], F32))
        mod_sb = ctx.enter_context(sb("mod_sb", [128, 96], F32))
        mtmp = ctx.enter_context(sb("mtmp", [128, 32], F32))
        sa_bf = ctx.enter_context(sb("sa_bf", [128, 16], BF16))
        ones = ctx.enter_context(sb("ones", [128, 128], BF16))
        x_db = ctx.enter_context(sb("x_db", [128, 2 * HCH * 512], BF16))
        xm_db = ctx.enter_context(sb("xm_db", [128, 2 * HCH * 512], BF16))
        xsq = ctx.enter_context(sb("xsq", [128, HCH * 512], BF16))
        stat = ctx.enter_context(sb("stat", [128, 7 * 512], F32))
        w_in_sb = ctx.enter_context(sb("w_in_sb", [128, HCH * E], BF16))
        binbc = ctx.enter_context(sb("binbc", [128, E], F32))
        zcb = ctx.enter_context(sb("zcb", [128, 2 * E], F32))
        zc_row = ctx.enter_context(sb("zc_row", [1, 2 * E], F32))
        rstdT = ctx.enter_context(sb("rstdT", [128, 2 * 128], F32))
        z_db = ctx.enter_context(sb("z_db", [128, 2 * E], BF16))
        eps_sb = ctx.enter_context(sb("eps_sb", [128, 1], F32))

        arena = ctx.enter_context(nc.psum_tensor("arena", [128, 4096], F32))

        def ps(off, n):
            return bass.AP(arena, off, [[4096, 128], [1, n]])

        def pz(off, n):
            return bass.AP(arena, off, [[4096, 1], [1, n]])

        PS_STAT = 0
        PS_MOD = 2048
        PS_ZA = 1024
        PS_ZB = 2048
        PS_ZC = 3072

        sems = _sems(ctx, nc, SEMN)
        S = Sched(nc, sems)

        SM, SQ, SVAR, SRSTD, SU0 = (i * 512 for i in range(5))
        SSTD = SQ

        t_ct = S.dma("gpsimd", ct_sb[:, :],
                     bass.AP(cts, 0, [[2, 128], [256, 8], [1, 2]]), inc="dct")
        t_xl = {}
        for tt in range(2):
            t_xl[tt] = S.dma(
                "gpsimd", x_db[:, (tt % 2) * HCH * 512:
                               (tt % 2 + 1) * HCH * 512],
                bass.AP(xrev, tt * 128 * HCH * 512,
                        [[HCH * 512, 128], [1, HCH * 512]]),
                inc="xl0" if tt % 2 == 0 else "xl1")
        t_silu = S.emit("scalar",
                        lambda sc: sc.activation(out=silu_c[:, :], in_=ct_sb[:, :],
                                                 func=AF.Silu),
                        waits=[t_ct], inc="ae")

        def init_consts(v):
            v.memset(ones[:, :], 1.0)
            return v.memset(eps_sb[:, :], EPS)
        t_ones = S.emit("vector", init_consts, inc="ve")

        t_modmm = {}
        t_macc = {}
        t_wm = {}
        for ch in range(HCH):
            half = (ch % 2) * 6 * D
            pmod = PS_MOD + (ch % 2) * 512
            t_wm[ch] = S.dma("sync", wmod_db[:, half:half + 6 * D],
                             bass.AP(wmod, ch * 128 * 6 * D,
                                     [[6 * D, 128], [1, 6 * D]]),
                             waits=[t_modmm.get(ch - 2)],
                             inc="wm0" if ch % 2 == 0 else "wm1")

            def mod_mm(t, ch=ch, half=half, pmod=pmod):
                inst = None
                for ic in range(48):
                    inst = t.matmul(ps(pmod + 2 * ic, 2),
                                    wmod_db[:, half + ic * 128:half + (ic + 1) * 128],
                                    silu_c[:, 2 * ch:2 * ch + 2],
                                    start=True, stop=True)
                return inst
            t_modmm[ch] = S.emit("tensor", mod_mm,
                                 waits=[t_silu if ch == 0 else None, t_wm[ch],
                                        t_macc.get(ch - 2)],
                                 inc="pe")

            def mod_acc(v, ch=ch, pmod=pmod):
                if ch == 0:
                    return v.tensor_copy(mod_sb[:, :], ps(pmod, 96))
                return v.tensor_tensor(mod_sb[:, :], mod_sb[:, :], ps(pmod, 96),
                                       ALU.add)
            t_macc[ch] = S.emit("vector", mod_acc, waits=[t_modmm[ch]], inc="ve")

        t_sabf = S.emit("vector",
                        lambda v: v.tensor_copy(sa_bf[:, :], mod_sb[:, 0:16]),
                        waits=[t_macc[HCH - 1]], inc="ve")

        def mod_post(v):
            cnt = [0]

            def step(inst):
                cnt[0] += 1
                inst.then_inc(sems["vq"], 1)
                v.wait_ge(sems["vq"], S.vq_base + cnt[0])
            for base in (32, 80):
                sl = mod_sb[:, base:base + 16]
                step(v.tensor_single_scalar(mtmp[:, 0:16], sl, 0.0, ALU.is_ge))
                step(v.tensor_scalar(mtmp[:, 0:16], mtmp[:, 0:16], 2.0, -1.0,
                                     ALU.mult, ALU.add))
                step(v.tensor_scalar(mtmp[:, 16:32], sl, -1.0, None, ALU.mult))
                step(v.tensor_tensor(mtmp[:, 16:32], sl, mtmp[:, 16:32], ALU.max))
                step(v.tensor_single_scalar(mtmp[:, 16:32], mtmp[:, 16:32], 0.25,
                                            ALU.max))
                step(v.tensor_tensor(sl, mtmp[:, 0:16], mtmp[:, 16:32], ALU.mult))
            S.vq_base += cnt[0]
            inst = None
            for base in (16, 64):
                inst = v.tensor_single_scalar(mod_sb[:, base:base + 16],
                                              mod_sb[:, base:base + 16], 1.0, ALU.add)
            return inst
        t_modpost = S.emit("vector", mod_post, waits=[t_macc[HCH - 1]], inc="ve")
        t_modst = S.dma("scalar", mod_out.ap(), mod_sb[:, :], waits=[t_modpost],
                        inc="st")

        t_win = {}
        for ch in range(HCH):
            t_win[ch] = S.dma("gpsimd", w_in_sb[:, ch * E:(ch + 1) * E],
                              bass.AP(w_inT, ch * 128 * E, [[E, 128], [1, E]]),
                              inc="dwi")
        t_bin = S.dma("sync", binbc[:, :], bass.AP(b_in, 0, [[0, 128], [1, E]]),
                      inc="dbb")

        t_zc = {}
        t_zcev = {}
        for k in range(4):
            b, half2 = divmod(k, 2)

            def zc_mm(t, b=b, half2=half2):
                inst = None
                for eq in range(2):
                    for ch in range(HCH):
                        inst = t.matmul(
                            pz(PS_ZC + eq * 512, 512),
                            sa_bf[:, ch * 2 + b:ch * 2 + b + 1],
                            w_in_sb[:, ch * E + (half2 * 2 + eq) * 512:
                                    ch * E + (half2 * 2 + eq) * 512 + 512],
                            start=(ch == 0), stop=(ch == HCH - 1))
                return inst
            t_zc[k] = S.emit("tensor", zc_mm,
                             waits=[t_sabf if k == 0 else None,
                                    t_win[HCH - 1] if k == 0 else None,
                                    t_zcev.get(k - 1)], inc="pe")
            t_zcev[k] = S.emit(
                "vector",
                lambda v, b=b, half2=half2: v.tensor_copy(
                    zc_row[0:1, b * E + half2 * 1024:b * E + half2 * 1024 + 1024],
                    pz(PS_ZC, 1024)),
                waits=[t_zc[k]], inc="ve")
        t_zcs = S.dma("sync", zcd.ap(), zc_row[0:1, :], waits=[t_zcev[3]],
                      inc="dbb")
        t_zcb = {}
        for b in range(2):
            t_zcb[b] = S.dma("sync", zcb[:, b * E:(b + 1) * E],
                             bass.AP(zcd, b * E, [[0, 128], [1, E]]),
                             waits=[t_zcs if b == 0 else None], inc="dbb")

        def zcb_add(v):
            inst = None
            for b in range(2):
                inst = v.tensor_tensor(zcb[:, b * E:(b + 1) * E],
                                       zcb[:, b * E:(b + 1) * E],
                                       binbc[:, :], ALU.add)
            return inst
        t_zcball = S.emit("vector", zcb_add, waits=[t_zcb[1], t_bin], inc="ve")

        t_xm = {}
        t_pe_stats = {}
        t_mq = {}
        t_std = {}
        t_rstr = {}
        t_zdma = {}
        t_zeva = {}
        t_zevb = {}
        t_mmb_tile = {}
        for tt in range(NTT):
            xs = (tt % 2) * HCH * 512
            if tt >= 2:
                t_xl[tt] = S.dma(
                    "gpsimd", x_db[:, xs:xs + HCH * 512],
                    bass.AP(xrev, tt * 128 * HCH * 512,
                            [[HCH * 512, 128], [1, HCH * 512]]),
                    waits=[t_xm.get((tt - 2, HCH - 1))],
                    inc="xl0" if tt % 2 == 0 else "xl1")
            t_sq = S.emit("scalar",
                          lambda sc, xs=xs: sc.activation(
                              out=xsq[:, :], in_=x_db[:, xs:xs + HCH * 512],
                              func=AF.Square),
                          waits=[t_xl[tt], t_pe_stats.get(tt - 1)], inc="ae")

            def stats_mm(t, xs=xs):
                inst = None
                for ch in range(HCH):
                    t.matmul(ps(PS_STAT, 512),
                             ones[:, :], x_db[:, xs + ch * 512: xs + ch * 512 + 512],
                             start=(ch == 0), stop=(ch == HCH - 1))
                for ch in range(HCH):
                    inst = t.matmul(ps(PS_STAT + 512, 512),
                                    ones[:, :], xsq[:, ch * 512:(ch + 1) * 512],
                                    start=(ch == 0), stop=(ch == HCH - 1))
                return inst
            t_pe_stats[tt] = S.emit("tensor", stats_mm,
                                    waits=[t_ones if tt == 0 else None, t_sq,
                                           t_mq.get(tt - 1)], inc="pe")

            def mq(v):
                v.tensor_scalar(stat[:, SM:SM + 512], ps(PS_STAT, 512),
                                1.0 / D, None, ALU.mult)
                v.tensor_scalar(stat[:, SQ:SQ + 512], ps(PS_STAT + 512, 512),
                                1.0 / D, None, ALU.mult)
                v.tensor_tensor(stat[:, SVAR:SVAR + 512], stat[:, SM:SM + 512],
                                stat[:, SM:SM + 512], ALU.mult)
                return v.tensor_tensor(stat[:, SVAR:SVAR + 512], stat[:, SQ:SQ + 512],
                                       stat[:, SVAR:SVAR + 512], ALU.subtract)
            t_mq[tt] = S.emit("vector", mq,
                              waits=[t_pe_stats[tt], t_std.get(tt - 1)], inc="ve")
            t_std[tt] = S.emit("scalar",
                               lambda sc: sc.activation(out=stat[:, SSTD:SSTD + 512],
                                                        in_=stat[:, SVAR:SVAR + 512],
                                                        func=AF.Sqrt, bias=eps_sb[:, 0:1]),
                               waits=[t_mq[tt]], inc="ae")
            bq = 0 if tt < NTT // 2 else 1
            for ch in range(HCH):
                def xm_fn(v, xs=xs, ch=ch, bq=bq):
                    col = 16 + ch * 2 + bq
                    v.tensor_tensor(stat[:, SU0:SU0 + 512],
                                    x_db[:, xs + ch * 512: xs + ch * 512 + 512],
                                    stat[:, SM:SM + 512], ALU.subtract)
                    return v.tensor_scalar(
                        xm_db[:, xs + ch * 512: xs + ch * 512 + 512],
                        stat[:, SU0:SU0 + 512],
                        mod_sb[:, col:col + 1], None, ALU.mult)
                t_xm[(tt, ch)] = S.emit(
                    "vector", xm_fn,
                    waits=[t_mq[tt] if ch == 0 else None,
                           t_modpost if (tt, ch) == (0, 0) else None,
                           (t_mmb_tile.get(tt - 2) if ch == 0 else None)],
                    inc="ve")

            def rstr_fn(v, tt=tt):
                inst = v.reciprocal(stat[:, SRSTD:SRSTD + 512],
                                    stat[:, SSTD:SSTD + 512])
                inst.then_inc(sems["vq"], 1)
                S.vq_base += 1
                v.wait_ge(sems["vq"], S.vq_base)
                par = (tt % 2) * 128
                for j in range(4):
                    for k in range(4):
                        inst = v.transpose(
                            rstdT[k * 32:(k + 1) * 32,
                                  par + j * 32: par + j * 32 + 32],
                            stat[0:32, SRSTD + j * 128 + k * 32:
                                 SRSTD + j * 128 + (k + 1) * 32])
                return inst
            t_rstr[tt] = S.emit("vector", rstr_fn, waits=[t_std[tt]], inc="ve")

            for jq in range(4):
                bq2 = tt // 4
                jt = (tt % 4) * 4 + jq
                gidx = bq2 * NJ + jt
                zs = (gidx % 2) * E
                j4 = jt % 4

                def inproj_a(t, xs=xs, j4=j4):
                    inst = None
                    for ch in range(HCH):
                        lhs = xm_db[:, xs + ch * 512 + j4 * 128:
                                    xs + ch * 512 + j4 * 128 + 128]
                        for eq in range(2):
                            inst = t.matmul(ps(PS_ZA + eq * 512, 512), lhs,
                                            w_in_sb[:, ch * E + eq * 512:
                                                    ch * E + eq * 512 + 512],
                                            start=(ch == 0), stop=(ch == HCH - 1))
                    return inst
                t_mma = S.emit("tensor", inproj_a,
                               waits=[t_xm[(tt, HCH - 1)] if jq == 0 else None,
                                      t_zeva.get(gidx - 1)], inc="pe")

                def inproj_b(t, xs=xs, j4=j4):
                    inst = None
                    for ch in range(HCH):
                        lhs = xm_db[:, xs + ch * 512 + j4 * 128:
                                    xs + ch * 512 + j4 * 128 + 128]
                        for eq in range(2):
                            inst = t.matmul(ps(PS_ZB + eq * 512, 512), lhs,
                                            w_in_sb[:, ch * E + (2 + eq) * 512:
                                                    ch * E + (2 + eq) * 512 + 512],
                                            start=(ch == 0), stop=(ch == HCH - 1))
                    return inst
                t_mmb = S.emit("tensor", inproj_b,
                               waits=[t_macc[HCH - 1] if gidx == 0 else None,
                                      t_zevb.get(gidx - 1)], inc="pe")
                t_mmb_tile[tt] = t_mmb

                def zev_a(v, zs=zs, tt=tt, j4=j4, bq2=bq2):
                    par = (tt % 2) * 128
                    rcol = rstdT[:, par + j4 * 32: par + j4 * 32 + 1]
                    return v.scalar_tensor_tensor(
                        z_db[:, zs:zs + 1024], ps(PS_ZA, 1024), rcol,
                        zcb[:, bq2 * E: bq2 * E + 1024], ALU.mult, ALU.add)
                t_zeva[gidx] = S.emit(
                    "vector", zev_a,
                    waits=[t_mma, t_zcball if gidx == 0 else None,
                           t_zdma.get(gidx - 2)], inc="ve")

                def zev_b(v, zs=zs, tt=tt, j4=j4, bq2=bq2):
                    par = (tt % 2) * 128
                    rcol = rstdT[:, par + j4 * 32: par + j4 * 32 + 1]
                    return v.scalar_tensor_tensor(
                        z_db[:, zs + 1024:zs + 2048], ps(PS_ZB, 1024), rcol,
                        zcb[:, bq2 * E + 1024: bq2 * E + 2048], ALU.mult,
                        ALU.add)
                t_zevb[gidx] = S.emit("vector", zev_b, waits=[t_mmb], inc="ve")
                t_zdma[gidx] = S.dma(
                    "sync", bass.AP(zout, (bq2 * T + jt * 128) * E,
                                    [[E, 128], [1, E]]),
                    z_db[:, zs:zs + E], waits=[t_zevb[gidx]],
                    inc="zst0" if gidx % 2 == 0 else "zst1")

        S.emit("sync", lambda e: None,
               waits=[S.tok("zst0"), S.tok("zst1")])
        S.emit("scalar", lambda e: None, waits=[S.tok("st")])

        with nc.Block() as blk:
            S.replay(blk)

    return nc


L2_SEMN = ["dzi", "dbb", "tp0", "tp1", "gst0", "gst1", "pe", "ve", "ae"]
NPAIR = 128
GPAIRS = 8
NGRP = NPAIR // GPAIRS


def build_l2():
    nc = bass.Bass("TRN2", target_bir_lowering=False, debug=False,
                   detect_race_conditions=False)
    zin = nc.dram_tensor("zin", [16 * NJ * 128, 256], BF16, kind="ExternalInput")
    wrevcq = nc.dram_tensor("wrevcq", [768, 256], F32, kind="ExternalInput")
    b_ab = nc.dram_tensor("b_ab", [1, 256], F32, kind="ExternalInput")
    gated = nc.dram_tensor("gated", [256 * 128, 256], BF16, kind="ExternalOutput")

    ctx = contextlib.ExitStack()
    with ctx:
        sb = nc.sbuf_tensor
        zin_sb = ctx.enter_context(sb("zin_sb", [128, 16 * NJ * 256], BF16))
        toep_db = ctx.enter_context(sb("toep_db", [128, 2 * 6 * GPAIRS * 128], BF16))
        bb_bc = ctx.enter_context(sb("bb_bc", [128, 256], F32))
        sig_sb = ctx.enter_context(sb("sig_sb", [128, 4 * 256], BF16))
        lin_sb = ctx.enter_context(sb("lin_sb", [128, 4 * 256], BF16))
        gbuf = ctx.enter_context(sb("gbuf", [128, 2 * 2 * GPAIRS * 256], BF16))
        arena = ctx.enter_context(nc.psum_tensor("arena", [128, 4096], F32))

        def ps(off, n):
            return bass.AP(arena, off, [[4096, 128], [1, n]])

        sems = _sems(ctx, nc, L2_SEMN)
        S = Sched(nc, sems)

        t_zi = None
        for b in range(16):
            t_zi = S.dma(
                "sync", bass.AP(zin_sb, b * NJ * 256,
                                [[16 * NJ * 256, 128], [256, NJ], [1, 256]]),
                bass.AP(zin, b * NJ * 128 * 256,
                        [[256, 128], [128 * 256, NJ], [1, 256]]),
                inc="dzi")
        t_bb = S.dma("sync", bb_bc[:, :], bass.AP(b_ab, 0, [[0, 128], [1, 256]]),
                     inc="dbb")

        t_mm = {}
        t_dve = {}
        t_st = {}
        t_tp = {}
        for g in range(NGRP):
            par = g % 2
            tslot = par * 6 * GPAIRS * 128
            last_mm_war = t_mm.get((g - 2, GPAIRS - 1))
            tok = None
            for half, rbase in ((0, 3 * GPAIRS * g), (1, 384 + 3 * GPAIRS * g)):
                for piece in range(1):
                    tok = S.dma(
                        "gpsimd",
                        bass.AP(toep_db, tslot + half * 3 * GPAIRS * 128 + piece * 24 * 128,
                                [[2 * 6 * GPAIRS * 128, 128], [128, 24], [1, 128]]),
                        bass.AP(wrevcq, (rbase + piece * 24) * 256,
                                [[1, 128], [256, 24], [1, 128]]),
                        waits=[last_mm_war if (half, piece) == (0, 0) else None],
                        inc="tp0" if par == 0 else "tp1")
            t_tp[g] = tok
            for pg in range(GPAIRS):
                p = g * GPAIRS + pg
                pp = p % 2
                pa, pb = pp * 1024, pp * 1024 + 512
                sa_c, sb_c = (pp * 2) * 256, (pp * 2 + 1) * 256

                def conv_mm(t, pg=pg, p=p, tslot=tslot, pa=pa, pb=pb):
                    inst = None
                    for half, po in ((0, pa), (1, pb)):
                        blk0 = tslot + half * 3 * GPAIRS * 128 + (pg * 3) * 128
                        ccol = p + half * 128
                        for q in range(3):
                            mv = bass.AP(zin_sb, ccol,
                                         [[16 * NJ * 256, 128], [256, NJ - q],
                                          [NJ * 256, 16]])
                            pso = bass.AP(arena, po + q * 16,
                                          [[4096, 128], [1, (NJ - q) * 16]])
                            inst = t.matmul(pso,
                                            toep_db[:, blk0 + q * 128:
                                                    blk0 + (q + 1) * 128],
                                            mv, start=(q == 0), stop=(q == 2))
                    return inst
                t_mm[(g, pg)] = S.emit(
                    "tensor", conv_mm,
                    waits=[t_zi if p == 0 else None,
                           t_tp[g] if pg == 0 else None,
                           t_dve.get((g, pg - 2)) if pg >= 2 else
                           (t_dve.get((g - 1, GPAIRS - 2 + pg)) if g >= 1 else None)],
                    inc="pe")

                def gate_act(sc, p=p, pa=pa, pb=pb, sa_c=sa_c, sb_c=sb_c):
                    sc.activation(out=sig_sb[:, sa_c:sa_c + 256],
                                  in_=ps(pa, 256), func=AF.Sigmoid,
                                  bias=bb_bc[:, p:p + 1])
                    return sc.activation(out=sig_sb[:, sb_c:sb_c + 256],
                                         in_=ps(pb, 256), func=AF.Sigmoid,
                                         bias=bb_bc[:, 128 + p:128 + p + 1])
                t_act = S.emit("scalar", gate_act,
                               waits=[t_mm[(g, pg)], t_bb if p == 0 else None],
                               inc="ae")

                def gate_dve(v, g=g, pg=pg, p=p, pa=pa, pb=pb, par=par,
                             sa_c=sa_c, sb_c=sb_c):
                    base = par * 2 * GPAIRS * 256
                    va = base + pg * 256
                    vb = base + (GPAIRS + pg) * 256
                    la, lb = sa_c, sb_c
                    v.tensor_scalar_add(lin_sb[:, la:la + 256], ps(pa, 256),
                                        bb_bc[:, p:p + 1])
                    v.tensor_scalar_add(lin_sb[:, lb:lb + 256], ps(pb, 256),
                                        bb_bc[:, 128 + p:128 + p + 1])
                    v.tensor_tensor(gbuf[:, va:va + 256], lin_sb[:, la:la + 256],
                                    sig_sb[:, sb_c:sb_c + 256], ALU.mult)
                    return v.tensor_tensor(gbuf[:, vb:vb + 256],
                                           lin_sb[:, lb:lb + 256],
                                           sig_sb[:, sa_c:sa_c + 256], ALU.mult)
                t_dve[(g, pg)] = S.emit(
                    "vector", gate_dve,
                    waits=[t_act, (t_st.get(g - 2) if pg == 0 else None)],
                    inc="ve")
            for half in range(2):
                t_st[g] = S.dma(
                    "sync",
                    bass.AP(gated, ((half * 128 + GPAIRS * g) * 128) * 256,
                            [[256, 128], [128 * 256, GPAIRS], [1, 256]]),
                    bass.AP(gbuf, par * 2 * GPAIRS * 256 + half * GPAIRS * 256,
                            [[2 * 2 * GPAIRS * 256, 128], [256, GPAIRS], [1, 256]]),
                    waits=[t_dve[(g, GPAIRS - 1)] if half == 0 else None],
                    inc="gst0" if par == 0 else "gst1")

        S.emit("sync", lambda e: None, waits=[S.tok("gst0"), S.tok("gst1")])

        with nc.Block() as blk:
            S.replay(blk)

    return nc


def host_prep_l2(z_by_core, inputs, core):
    c = core
    za = np.stack([z[:, 128 * c:128 * c + 128] for z in z_by_core])
    zb = np.stack([z[:, 1024 + 128 * c:1024 + 128 * c + 128] for z in z_by_core])
    za = za.reshape(8, 2, NJ, 128, 128).reshape(16, NJ, 128, 128)
    zb = zb.reshape(8, 2, NJ, 128, 128).reshape(16, NJ, 128, 128)
    zin = np.concatenate([za, zb], axis=3).reshape(16 * NJ * 128, 256)
    w_a = np.asarray(inputs["L0_attn_w_a"])[:, 0, :]
    w_b = np.asarray(inputs["L0_attn_w_b"])[:, 0, :]
    wa = w_a[128 * c:128 * c + 128]
    wb = w_b[128 * c:128 * c + 128]
    wrev = np.zeros((256, 512), np.float32)
    m = np.arange(127, 383)
    wrev[0:128, m] = wa[:, 382 - m]
    wrev[128:256, m] = wb[:, 382 - m]
    wrevcq = np.zeros((768, 256), np.float32)
    for q in range(3):
        wrevcq[q::3, :] = wrev[:, 128 * q:128 * q + 256]
    b_ab = np.concatenate([
        np.asarray(inputs["L0_attn_b_a"])[128 * c:128 * c + 128],
        np.asarray(inputs["L0_attn_b_b"])[128 * c:128 * c + 128]])[None, :]
    return {"zin": np.ascontiguousarray(zin),
            "wrevcq": wrevcq,
            "b_ab": np.ascontiguousarray(b_ab).astype(np.float32)}


L3_SEMN = ["dwo", "dbo", "dmo", "xb0", "xb1", "xs0", "xs1", "xn0", "xn1",
           "x1s0", "x1s1", "dmx", "pe", "ve", "ae", "vq"]


def build_l3():
    nc = bass.Bass("TRN2", target_bir_lowering=False, debug=False,
                   detect_race_conditions=False)
    g3 = nc.dram_tensor("g3", [E, TOK], BF16, kind="ExternalInput")
    xn = nc.dram_tensor("xn", [D, TOK], F32, kind="ExternalInput")
    w_outT = nc.dram_tensor("w_outT", [E, D], F32, kind="ExternalInput")
    b_outc = nc.dram_tensor("b_outc", [128, 16], F32, kind="ExternalInput")
    mod_in = nc.dram_tensor("mod_in", [128, 96], F32, kind="ExternalInput")
    x1o = nc.dram_tensor("x1o", [D, TOK], BF16, kind="ExternalOutput")
    max1o = nc.dram_tensor("max1o", [1, 1], F32, kind="ExternalOutput")
    mxs = nc.dram_tensor("mxs", [1, 128], F32, kind="Internal")

    ctx = contextlib.ExitStack()
    with ctx:
        sb = nc.sbuf_tensor
        wo_sb = ctx.enter_context(sb("wo_sb", [128, ECH * D], BF16))
        cat_db = ctx.enter_context(sb("cat_db", [128, 2 * ECH * 512], BF16))
        x0_db = ctx.enter_context(sb("x0_db", [128, 2 * HCH * 512], F32))
        x1_db = ctx.enter_context(sb("x1_db", [128, 2 * HCH * 512], BF16))
        xsq = ctx.enter_context(sb("xsq2", [128, HCH * 512], BF16))
        stat = ctx.enter_context(sb("stat3", [128, 8 * 512], F32))
        mod_sb = ctx.enter_context(sb("mod_sb3", [128, 96], F32))
        bo_sb = ctx.enter_context(sb("bo_sb", [128, 16], F32))
        gab_sb = ctx.enter_context(sb("gab_sb", [128, 16], F32))
        ones = ctx.enter_context(sb("ones3", [128, 128], BF16))
        eps_sb = ctx.enter_context(sb("eps3", [128, 1], F32))
        mxcol = ctx.enter_context(sb("mxcol", [128, 4], F32))
        mxrow = ctx.enter_context(sb("mxrow", [1, 128], F32))
        mx1 = ctx.enter_context(sb("mx1", [1, 1], F32))
        arena = ctx.enter_context(nc.psum_tensor("arena3", [128, 4096], F32))

        def ps(off, n):
            return bass.AP(arena, off, [[4096, 128], [1, n]])

        PS_O = 0
        PS_STAT = 1024

        sems = _sems(ctx, nc, L3_SEMN)
        S = Sched(nc, sems)
        SM, SQ, SVAR, SRSTD = (i * 512 for i in range(4))
        SSTD = SQ
        SU0 = SM
        SU1 = SU0

        def init_consts(v):
            v.memset(ones[:, :], 1.0)
            v.memset(mxcol[:, :], 0.0)
            return v.memset(eps_sb[:, :], EPS)
        t_ones = S.emit("vector", init_consts, inc="ve")

        t_wo = None
        for ec in range(ECH):
            t_wo = S.dma("gpsimd", wo_sb[:, ec * D:(ec + 1) * D],
                         bass.AP(w_outT, ec * 128 * D, [[D, 128], [1, D]]),
                         inc="dwo")
        t_bo = S.dma("sync", bo_sb[:, :], b_outc.ap(), inc="dbo")
        t_mo = S.dma("sync", mod_sb[:, :], mod_in.ap(), inc="dmo")
        t_gab = S.emit("vector",
                       lambda v: v.tensor_tensor(gab_sb[:, :], mod_sb[:, 32:48],
                                                 bo_sb[:, :], ALU.mult),
                       waits=[t_bo, t_mo], inc="ve")

        t_xbar = {}
        t_mm = {}
        t_ev = {}
        t_sq = {}
        t_mq = {}
        t_u = {}
        t_x1s = {}
        for tt in range(NTT):
            cslot = (tt % 2) * ECH * 512
            xslot = (tt % 2) * HCH * 512
            war = t_mm.get((tt - 2, HCH - 1))
            tokb = toks = None
            for ec in range(ECH):
                eng = "sync" if ec % 2 == 0 else "scalar"
                semn = ("xb0" if tt % 2 == 0 else "xb1") if eng == "sync" else \
                       ("xs0" if tt % 2 == 0 else "xs1")
                tok = S.dma(eng, bass.AP(cat_db, cslot + ec * 512,
                                         [[2 * ECH * 512, 128], [1, 512]]),
                            bass.AP(g3, (ec * 128) * TOK + tt * 512,
                                    [[TOK, 128], [1, 512]]),
                            waits=[war if ec in (0, 1) else None],
                            inc=semn)
                if eng == "sync":
                    tokb = tok
                else:
                    toks = tok
            t_xbar[tt] = (tokb, toks)
            for ch in range(HCH):
                t_x = S.dma("gpsimd",
                            x0_db[:, xslot + ch * 512: xslot + ch * 512 + 512],
                            bass.AP(xn, ch * 128 * TOK + tt * 512,
                                    [[TOK, 128], [1, 512]]),
                            waits=[t_ev.get((tt - 2, HCH - 1)) if ch == 0 else None],
                            inc="xn0" if tt % 2 == 0 else "xn1")
                if ch == HCH - 1:
                    t_xn = t_x
            for hc in range(HCH):
                po = PS_O + ((tt * HCH + hc) % 2) * 512

                def op_mm(t, cslot=cslot, hc=hc, po=po):
                    inst = None
                    for ec in range(ECH):
                        inst = t.matmul(ps(po, 512),
                                        wo_sb[:, ec * D + hc * 128:
                                              ec * D + hc * 128 + 128],
                                        cat_db[:, cslot + ec * 512:
                                               cslot + ec * 512 + 512],
                                        start=(ec == 0), stop=(ec == ECH - 1))
                    return inst
                t_mm[(tt, hc)] = S.emit(
                    "tensor", op_mm,
                    waits=[t_xbar[tt][0] if hc == 0 else None,
                           t_xbar[tt][1] if hc == 0 else None,
                           t_wo if (tt, hc) == (0, 0) else None,
                           t_ev.get((tt * HCH + hc - 2) if False else
                                    ((tt * HCH + hc - 2) // HCH,
                                     (tt * HCH + hc - 2) % HCH))],
                    inc="pe")

                def op_ev(v, tt=tt, hc=hc, po=po, xslot=xslot):
                    bq = 0 if tt < NTT // 2 else 1
                    col = hc * 2 + bq
                    v.tensor_scalar(stat[:, SU0:SU0 + 512], ps(po, 512),
                                    mod_sb[:, 32 + col:32 + col + 1],
                                    gab_sb[:, col:col + 1], ALU.mult, ALU.add)
                    return v.tensor_tensor(
                        x1_db[:, xslot + hc * 512: xslot + hc * 512 + 512],
                        stat[:, SU0:SU0 + 512],
                        x0_db[:, xslot + hc * 512: xslot + hc * 512 + 512],
                        ALU.add)
                t_ev[(tt, hc)] = S.emit(
                    "vector", op_ev,
                    waits=[t_mm[(tt, hc)], t_gab if (tt, hc) == (0, 0) else None,
                           t_xn if hc == 0 else None,
                           (t_x1s.get((tt - 2, HCH - 1)) if hc == 0 else None),
                           (t_sq.get(tt - 2) if hc == 0 else None)],
                    inc="ve")
                t_x1s[(tt, hc)] = S.dma(
                    "gpsimd", bass.AP(x1o, hc * 128 * TOK + tt * 512,
                                      [[TOK, 128], [1, 512]]),
                    x1_db[:, xslot + hc * 512: xslot + hc * 512 + 512],
                    waits=[t_ev[(tt, hc)]],
                    inc="x1s0" if tt % 2 == 0 else "x1s1")
            t_sq[tt] = S.emit("scalar",
                              lambda sc, xslot=xslot: sc.activation(
                                  out=xsq[:, :],
                                  in_=x1_db[:, xslot:xslot + HCH * 512],
                                  func=AF.Square),
                              waits=[t_ev[(tt, HCH - 1)],
                                     t_stq if tt >= 1 else None],
                              inc="ae")

            def stats_mm(t, xslot=xslot):
                inst = None
                for ch in range(HCH):
                    t.matmul(ps(PS_STAT, 512), ones[:, :],
                             x1_db[:, xslot + ch * 512: xslot + ch * 512 + 512],
                             start=(ch == 0), stop=(ch == HCH - 1))
                for ch in range(HCH):
                    inst = t.matmul(ps(PS_STAT + 512, 512), ones[:, :],
                                    xsq[:, ch * 512:(ch + 1) * 512],
                                    start=(ch == 0), stop=(ch == HCH - 1))
                return inst
            t_stq = S.emit("tensor", stats_mm,
                           waits=[t_ones if tt == 0 else None, t_sq[tt],
                                  t_mq.get(tt - 1)], inc="pe")

            def mq(v):
                v.tensor_scalar(stat[:, SM:SM + 512], ps(PS_STAT, 512),
                                1.0 / D, None, ALU.mult)
                v.tensor_scalar(stat[:, SQ:SQ + 512], ps(PS_STAT + 512, 512),
                                1.0 / D, None, ALU.mult)
                v.tensor_tensor(stat[:, SVAR:SVAR + 512], stat[:, SM:SM + 512],
                                stat[:, SM:SM + 512], ALU.mult)
                return v.tensor_tensor(stat[:, SVAR:SVAR + 512],
                                       stat[:, SQ:SQ + 512],
                                       stat[:, SVAR:SVAR + 512], ALU.subtract)
            t_mq[tt] = S.emit("vector", mq, waits=[t_stq, t_std if tt >= 1 else None],
                              inc="ve")
            t_std = S.emit("scalar",
                           lambda sc: sc.activation(out=stat[:, SSTD:SSTD + 512],
                                                    in_=stat[:, SVAR:SVAR + 512],
                                                    func=AF.Sqrt,
                                                    bias=eps_sb[:, 0:1]),
                           waits=[t_mq[tt]], inc="ae")

            def rstd_mrstd(v):
                v.reciprocal(stat[:, SRSTD:SRSTD + 512], stat[:, SSTD:SSTD + 512])
                return v.tensor_tensor(stat[:, SVAR:SVAR + 512],
                                       stat[:, SM:SM + 512],
                                       stat[:, SRSTD:SRSTD + 512], ALU.mult)
            t_rs = S.emit("vector", rstd_mrstd, waits=[t_std], inc="ve")
            bq = 0 if tt < NTT // 2 else 1
            for ch in range(HCH):
                su = SU1 if ch % 2 == 0 else SU1 + 512

                kidx = tt * HCH + ch

                def h2max(v, xslot=xslot, ch=ch, su=su, bq=bq, kidx=kidx):
                    col = ch * 2 + bq
                    if kidx >= 2:
                        v.tensor_tensor(mxcol[:, 0:1], mxcol[:, 0:1],
                                        mxcol[:, 1 + kidx % 2:2 + kidx % 2],
                                        ALU.max)
                    v.tensor_tensor(stat[:, su:su + 512],
                                    x1_db[:, xslot + ch * 512:
                                          xslot + ch * 512 + 512],
                                    stat[:, SRSTD:SRSTD + 512], ALU.mult)
                    v.tensor_tensor(stat[:, su:su + 512], stat[:, su:su + 512],
                                    stat[:, SVAR:SVAR + 512], ALU.subtract)
                    v.tensor_scalar(stat[:, su:su + 512], stat[:, su:su + 512],
                                    mod_sb[:, 64 + col:64 + col + 1],
                                    mod_sb[:, 48 + col:48 + col + 1],
                                    ALU.mult, ALU.add)
                    return v.reduce_max(mxcol[:, 1 + kidx % 2:2 + kidx % 2],
                                        stat[:, su:su + 512],
                                        axis=AX.X, apply_absolute_value=True)
                t_u[(tt, ch)] = S.emit("vector", h2max,
                                       waits=[t_rs if ch == 0 else None], inc="ve")

        def fin_acc(v):
            for kl in (1, 2):
                inst = v.tensor_tensor(mxcol[:, 0:1], mxcol[:, 0:1],
                                       mxcol[:, kl:kl + 1], ALU.max)
                inst.then_inc(sems["vq"], 1)
                S.vq_base += 1
                v.wait_ge(sems["vq"], S.vq_base)
            return v.tensor_copy(mxcol[:, 3:4], mxcol[:, 0:1])
        t_fa = S.emit("vector", fin_acc, waits=[t_u[(NTT - 1, HCH - 1)]], inc="ve")
        t_m1 = S.dma("gpsimd", bass.AP(mxs, 0, [[1, 128]]),
                     bass.AP(mxcol, 0, [[4, 128]]),
                     waits=[t_fa], inc="dmx")
        t_m2 = S.dma("gpsimd", mxrow[:, :], mxs.ap(), waits=[t_m1], inc="dmx")
        t_m3 = S.emit("vector",
                      lambda v: v.reduce_max(mx1[:, :], mxrow[:, :], axis=AX.X),
                      waits=[t_m2], inc="ve")
        t_m4 = S.dma("gpsimd", max1o.ap(), mx1[:, :], waits=[t_m3], inc="dmx")

        S.emit("gpsimd", lambda e: None,
               waits=[S.tok("x1s0"), S.tok("x1s1"), S.tok("dmx")])

        with nc.Block() as blk:
            S.replay(blk)

    return nc


def host_prep_l3(gated_by_core, inputs, core, mod_by_core):
    m = core
    g3 = np.zeros((E, BPC, NJ, 128), dtype=BFNP)
    for k in range(8):
        gk = np.asarray(gated_by_core[k]).reshape(256, 128, NJ, 16)
        for half in range(2):
            sl = gk[half * 128:half * 128 + 128]
            e0 = half * 1024 + 128 * k
            v = sl[:, :, :, 2 * m:2 * m + 2]
            g3[e0:e0 + 128] = v.transpose(0, 3, 2, 1)
    g3 = g3.reshape(E, TOK)
    b0 = m * BPC
    xs = np.asarray(inputs["L0_attn_x"][b0:b0 + BPC])
    xnr = np.ascontiguousarray(xs.transpose(2, 0, 1).reshape(D, TOK)).astype(np.float32)
    b_out = np.asarray(inputs["L0_attn_b_out"]).reshape(HCH, 128).T
    b_outc = np.repeat(b_out, 2, axis=1)
    return {"g3": np.ascontiguousarray(g3),
            "xn": xnr,
            "w_outT": np.ascontiguousarray(np.asarray(inputs["L0_attn_w_out"]).T).astype(np.float32),
            "b_outc": np.ascontiguousarray(b_outc).astype(np.float32),
            "mod_in": np.asarray(mod_by_core[m]).astype(np.float32)}


L4_SEMN = ["dx1a", "dx1b", "dmo", "dmx", "dsc", "dwr", "dm2", "dcl",
           "dws0", "dws1", "dwt0", "dwt1",
           "gs0", "gs1", "pe", "ve", "ae", "vq"]


def build_l4(stage=3):
    nc = bass.Bass("TRN2", target_bir_lowering=False, debug=False,
                   detect_race_conditions=False, dynamic_dma_scratch_size=8192)
    x1 = nc.dram_tensor("x1", [NTT * 128, HCH * 512], BF16,
                        kind="ExternalInput")
    mod_in = nc.dram_tensor("mod_in", [128, 96], F32, kind="ExternalInput")
    max1all = nc.dram_tensor("max1all", [1, 8], F32, kind="ExternalInput")
    clip_x = nc.dram_tensor("clip_x", [1, 1], F32, kind="ExternalInput")
    w_gT = nc.dram_tensor("w_gT", [D, II], F32, kind="ExternalInput")
    w_uT = nc.dram_tensor("w_uT", [D, II], F32, kind="ExternalInput")
    w_g = nc.dram_tensor("w_g", [II, D], F32, kind="ExternalInput")
    w_u = nc.dram_tensor("w_u", [II, D], F32, kind="ExternalInput")
    gout = nc.dram_tensor("gout", [II, TOK], BF16, kind="ExternalOutput")
    max2o = nc.dram_tensor("max2o", [1, 1], F32, kind="ExternalOutput")

    ctx = contextlib.ExitStack()
    with ctx:
        sb = nc.sbuf_tensor
        x1_db = ctx.enter_context(sb("x1db4", [128, 2 * HCH * 512], BF16))
        scratch8 = ctx.enter_context(sb("scr84", [128, HCH * 512], BF16))
        xsq = scratch8
        stat = ctx.enter_context(sb("stat4", [128, 4 * 512], F32))
        xq_sb = ctx.enter_context(sb("xq_sb", [128, HCH * TOK], BF16))
        codes = ctx.enter_context(sb("codes4", [128, 2 * HCH * II], BF16))
        wload = ctx.enter_context(sb("wload", [128, 2 * II], F32))
        wscales = ctx.enter_context(sb("wscales", [128, 64], F32))
        wsrec = ctx.enter_context(sb("wsrec", [128, 64], F32))
        wrowT = ctx.enter_context(sb("wrowT", [64, 128], F32))
        winv_bc = ctx.enter_context(sb("winv_bc", [128, II], BF16))
        oscl = ctx.enter_context(sb("oscl4", [128, 64], F32))
        mod_sb = ctx.enter_context(sb("mod_sb4", [128, 96], F32))
        sclq = ctx.enter_context(sb("sclq", [128, 16], F32))
        biasq = ctx.enter_context(sb("biasq", [128, 16], F32))
        sc_row = ctx.enter_context(sb("sc_row", [1, 8], F32))
        sc_one = ctx.enter_context(sb("sc_one", [1, 4], F32))
        s1bc = ctx.enter_context(sb("s1bc", [128, 2], F32))
        ones = ctx.enter_context(sb("ones4", [128, 128], BF16))
        eps_sb = ctx.enter_context(sb("eps4", [128, 1], F32))
        f16t = ctx.enter_context(sb("f16t4", [128, 512], F16))
        sg_sb = scratch8
        gtile_base = 2048
        mxcol = ctx.enter_context(sb("mxcol4", [128, 4], F32))
        mxrow = ctx.enter_context(sb("mxrow4", [1, 128], F32))
        mx1 = ctx.enter_context(sb("mx14", [1, 1], F32))
        arena = ctx.enter_context(nc.psum_tensor("arena4", [128, 4096], F32))
        scd = nc.dram_tensor("scd4", [1, 4], F32, kind="Internal")
        wrd = nc.dram_tensor("wrd4", [64, 128], F32, kind="Internal")
        mxs = nc.dram_tensor("mxs4", [1, 128], F32, kind="Internal")

        def ps(off, n):
            return bass.AP(arena, off, [[4096, 128], [1, n]])

        PS_G = 0
        PS_U = 2048
        PS_STAT = 0
        PS_B = 2048

        sems = _sems(ctx, nc, L4_SEMN)
        S = Sched(nc, sems)
        SM, SQ, SVAR, SRSTD = (i * 512 for i in range(4))
        SSTD = SQ
        SU0 = SM
        SU1 = SU0

        def init_consts(v):
            v.memset(ones[:, :], 1.0)
            v.memset(mxcol[:, :], 0.0)
            v.memset(wscales[:, :], 1.0)
            v.memset(wsrec[:, :], 1.0)
            return v.memset(eps_sb[:, :], EPS)
        t_ones = S.emit("vector", init_consts, inc="ve")

        t_mo = S.dma("sync", mod_sb[:, :], mod_in.ap(), inc="dmo")
        t_mxa = S.dma("sync", sc_row[:, :], max1all.ap(), inc="dmx")
        t_cl = S.dma("sync", sc_one[:, 3:4], clip_x.ap(), inc="dcl")

        def s1chain_v(v):
            return v.reduce_max(sc_one[:, 0:1], sc_row[:, :], axis=AX.X)
        t_gm = S.emit("vector", s1chain_v, waits=[t_mxa], inc="ve")
        t_sg = S.emit("scalar",
                      lambda sc: sc.activation(out=sc_one[:, 2:3], in_=sc_one[:, 3:4],
                                               func=AF.Sigmoid),
                      waits=[t_cl], inc="ae")

        def s1fin(v):
            cnt = [0]

            def stp(inst):
                cnt[0] += 1
                inst.then_inc(sems["vq"], 1)
                v.wait_ge(sems["vq"], S.vq_base + cnt[0])
            stp(v.tensor_scalar(sc_one[:, 0:1], sc_one[:, 0:1], sc_one[:, 2:3],
                                1.0 / QMAX, ALU.mult, ALU.mult))
            stp(v.reciprocal(sc_one[:, 1:2], sc_one[:, 0:1]))
            S.vq_base += cnt[0]
            return v.tensor_copy(sc_one[:, 2:3], sc_one[:, 1:2])
        t_s1 = S.emit("vector", s1fin, waits=[t_gm, t_sg], inc="ve")
        t_sd = S.dma("gpsimd", scd.ap(), sc_one[:, 0:4], waits=[t_s1], inc="dsc")
        t_s1b = S.dma("gpsimd", s1bc[:, :],
                      bass.AP(scd, 0, [[0, 128], [1, 2]]), waits=[t_sd], inc="dsc")

        def qscale(v):
            v.tensor_scalar(sclq[:, :], mod_sb[:, 64:80], s1bc[:, 1:2], None,
                            ALU.mult)
            return v.tensor_scalar(biasq[:, :], mod_sb[:, 48:64], s1bc[:, 1:2],
                                   None, ALU.mult)
        t_qs = S.emit("vector", qscale, waits=[t_s1b, t_mo], inc="ve")

        t_ws = {}
        t_wsc = {}
        for w in range(2):
            wnat = w_g if w == 0 else w_u
            for icp in range(ICH // 2):
                k = w * (ICH // 2) + icp
                par = k % 2
                t_ld = S.dma("sync", wload[:, par * II:par * II + 2 * D],
                             bass.AP(wnat, 2 * icp * 128 * D,
                                     [[D, 128], [128 * D, 2], [1, D]]),
                             waits=[t_wsc.get(k - 2)],
                             inc="dws0" if par == 0 else "dws1")

                def wsc_fn(v, w=w, icp=icp, par=par):
                    col = w * ICH + 2 * icp
                    v.reduce_max(wscales[:, col:col + 1],
                                 wload[:, par * II:par * II + D], axis=AX.X,
                                 apply_absolute_value=True)
                    return v.reduce_max(wscales[:, col + 1:col + 2],
                                        wload[:, par * II + D:par * II + 2 * D],
                                        axis=AX.X, apply_absolute_value=True)
                t_wsc[k] = S.emit("vector", wsc_fn, waits=[t_ld], inc="ve")
        def ws_tail(v):
            cnt = [0]

            def stp(inst):
                cnt[0] += 1
                inst.then_inc(sems["vq"], 1)
                v.wait_ge(sems["vq"], S.vq_base + cnt[0])
            stp(v.tensor_scalar(oscl[:, :], wscales[:, :], s1bc[:, 0:1],
                                1.0 / QMAX, ALU.mult, ALU.mult))
            stp(v.reciprocal(wsrec[:, :], wscales[:, :]))
            stp(v.tensor_single_scalar(wsrec[:, :], wsrec[:, :], QMAX, ALU.mult))
            S.vq_base += cnt[0]
            return v.tensor_copy(mxcol[:, 3:4], wsrec[:, 0:1])
        t_oscl = S.emit("vector", ws_tail,
                        waits=[t_wsc[ICH - 1], t_s1b], inc="ve")
        def wtr(v):
            inst = None
            for pb in range(4):
                for cb in range(2):
                    inst = v.transpose(wrowT[cb * 32:(cb + 1) * 32,
                                             pb * 32:(pb + 1) * 32],
                                       wsrec[pb * 32:(pb + 1) * 32,
                                             cb * 32:(cb + 1) * 32])
            return inst
        t_wtr = S.emit("vector", wtr, waits=[t_wsc[ICH - 1]], inc="ve")

        t_code = {}
        t_bc = {}
        t_wrd = S.dma("gpsimd", bass.AP(wrd, 0, [[128, 64], [1, 128]]),
                      bass.AP(wrowT, 0, [[128, 64], [1, 128]]),
                      waits=[t_wtr], inc="dwr")
        cidx = 0
        for w in range(2):
            wT = w_gT if w == 0 else w_uT
            t_bc[w] = S.dma(
                "gpsimd", winv_bc[:, :],
                bass.AP(wrd, (w * ICH) * 128, [[0, 128], [128, ICH], [1, 128]]),
                waits=[t_wrd, t_code.get(cidx - 1)],
                inc="dwr")
            for ch in range(HCH):
                par = cidx % 2
                t_wt = S.dma("sync", wload[:, par * II:(par + 1) * II],
                             bass.AP(wT, ch * 128 * II, [[II, 128], [1, II]]),
                             waits=[t_code.get(cidx - 2),
                                    t_wsc[ICH - 1] if cidx == 0 else None],
                             inc="dwt0" if par == 0 else "dwt1")

                def code_fn(v, w=w, ch=ch, par=par):
                    base = (w * HCH + ch) * II
                    dst = codes[:, base:base + II]
                    v.tensor_tensor(dst, wload[:, par * II:(par + 1) * II],
                                    winv_bc[:, :], ALU.mult)
                    inst = None
                    for off in range(0, II, 512):
                        wd = min(512, II - off)
                        v.tensor_single_scalar(f16t[:, 0:wd],
                                               dst[:, off:off + wd], MAGIC,
                                               ALU.add)
                        inst = v.tensor_single_scalar(dst[:, off:off + wd],
                                                      f16t[:, 0:wd], MAGIC,
                                                      ALU.subtract)
                    return inst
                t_code[cidx] = S.emit(
                    "vector", code_fn,
                    waits=[t_wt, t_bc[w] if ch == 0 else None],
                    inc="ve")
                cidx += 1

        if stage == 0:
            S.emit("gpsimd", lambda e: None, waits=[S.tok("dwr")])
            with nc.Block() as blk:
                S.replay(blk)
            return nc
        t_x1l = {}
        t_xq = {}
        t_sq = {}
        t_mq = {}
        t_stq = None
        t_std = None
        for tt in range(NTT):
            xslot = (tt % 2) * HCH * 512
            t_x1l[tt] = S.dma(
                "gpsimd", x1_db[:, xslot:xslot + HCH * 512],
                bass.AP(x1, tt * 128 * HCH * 512,
                        [[HCH * 512, 128], [1, HCH * 512]]),
                waits=[t_xq.get((tt - 2, HCH - 1))],
                inc="dx1a" if tt % 2 == 0 else "dx1b")
            if stage == 9:
                continue
            t_sq[tt] = S.emit("scalar",
                              lambda sc, xslot=xslot: sc.activation(
                                  out=xsq[:, :],
                                  in_=x1_db[:, xslot:xslot + HCH * 512],
                                  func=AF.Square),
                              waits=[t_x1l[tt], t_stq if tt >= 1 else None],
                              inc="ae")
            if stage == 8:
                continue

            def stats_mm(t, xslot=xslot):
                inst = None
                for ch in range(HCH):
                    t.matmul(ps(PS_STAT, 512), ones[:, :],
                             x1_db[:, xslot + ch * 512: xslot + ch * 512 + 512],
                             start=(ch == 0), stop=(ch == HCH - 1))
                for ch in range(HCH):
                    inst = t.matmul(ps(PS_STAT + 512, 512), ones[:, :],
                                    xsq[:, ch * 512:(ch + 1) * 512],
                                    start=(ch == 0), stop=(ch == HCH - 1))
                return inst
            t_stq = S.emit("tensor", stats_mm,
                           waits=[t_ones if tt == 0 else None, t_sq[tt],
                                  t_mq.get(tt - 1),
                                  t_bc[1] if tt == 0 else None],
                           inc="pe")

            def mq(v):
                v.tensor_scalar(stat[:, SM:SM + 512], ps(PS_STAT, 512),
                                1.0 / D, None, ALU.mult)
                v.tensor_scalar(stat[:, SQ:SQ + 512], ps(PS_STAT + 512, 512),
                                1.0 / D, None, ALU.mult)
                v.tensor_tensor(stat[:, SVAR:SVAR + 512], stat[:, SM:SM + 512],
                                stat[:, SM:SM + 512], ALU.mult)
                return v.tensor_tensor(stat[:, SVAR:SVAR + 512],
                                       stat[:, SQ:SQ + 512],
                                       stat[:, SVAR:SVAR + 512], ALU.subtract)
            if stage == 10:
                continue
            t_mq[tt] = S.emit("vector", mq,
                              waits=[t_stq, t_std if tt >= 1 else None], inc="ve")
            t_std = S.emit("scalar",
                           lambda sc: sc.activation(out=stat[:, SSTD:SSTD + 512],
                                                    in_=stat[:, SVAR:SVAR + 512],
                                                    func=AF.Sqrt,
                                                    bias=eps_sb[:, 0:1]),
                           waits=[t_mq[tt]], inc="ae")

            def rstd_mrstd(v):
                v.reciprocal(stat[:, SRSTD:SRSTD + 512], stat[:, SSTD:SSTD + 512])
                return v.tensor_tensor(stat[:, SVAR:SVAR + 512],
                                       stat[:, SM:SM + 512],
                                       stat[:, SRSTD:SRSTD + 512], ALU.mult)
            t_rs = S.emit("vector", rstd_mrstd, waits=[t_std], inc="ve")
            bq = 0 if tt < NTT // 2 else 1
            for ch in range(HCH):
                su = SU0 if ch % 2 == 0 else SU1

                def xq_fn(v, xslot=xslot, tt=tt, ch=ch, su=su, bq=bq):
                    col = ch * 2 + bq
                    dst = xq_sb[:, ch * TOK + tt * 512: ch * TOK + tt * 512 + 512]
                    v.tensor_tensor(stat[:, su:su + 512],
                                    x1_db[:, xslot + ch * 512:
                                          xslot + ch * 512 + 512],
                                    stat[:, SRSTD:SRSTD + 512], ALU.mult)
                    v.tensor_tensor(stat[:, su:su + 512], stat[:, su:su + 512],
                                    stat[:, SVAR:SVAR + 512], ALU.subtract)
                    v.tensor_scalar(stat[:, su:su + 512], stat[:, su:su + 512],
                                    sclq[:, col:col + 1],
                                    biasq[:, col:col + 1],
                                    ALU.mult, ALU.add)
                    v.tensor_scalar(stat[:, su:su + 512], stat[:, su:su + 512],
                                    CLIP_LO, CLIP_HI, ALU.max, ALU.min)
                    v.tensor_single_scalar(f16t[:, 0:512], stat[:, su:su + 512],
                                           MAGIC, ALU.add)
                    return v.tensor_single_scalar(dst, f16t[:, 0:512], MAGIC,
                                                  ALU.subtract)
                if stage != 11:
                    t_xq[(tt, ch)] = S.emit(
                        "vector", xq_fn,
                        waits=[t_rs if ch == 0 else None,
                               t_qs if (tt, ch) == (0, 0) else None],
                        inc="ve")

        if stage in (1, 8, 9, 10, 11):
            S.emit("gpsimd", lambda e: None, waits=[S.tok("dx1a"), S.tok("dx1b")])
            with nc.Block() as blk:
                S.replay(blk)
            return nc
        t_gst = {}
        t_gd = {}
        slot = 0
        lastq = t_xq.get((NTT - 1, HCH - 1))
        for th in range(2 if stage == 2 else 4):
            for ic in range(ICH):
                spar = slot % 2
                PSG = spar * 1024
                PSU = 2048 + spar * 1024

                def gu_mm(t, th=th, ic=ic, PSG=PSG, PSU=PSU):
                    inst = None
                    for w in range(2):
                        pbase = PSG if w == 0 else PSU
                        for ch in range(HCH):
                            lhs = codes[:, (w * HCH + ch) * II + ic * 128:
                                        (w * HCH + ch) * II + (ic + 1) * 128]
                            for q in range(2):
                                tt = th * 2 + q
                                mvx = xq_sb[:, ch * TOK + tt * 512:
                                            ch * TOK + tt * 512 + 512]
                                inst = t.matmul(ps(pbase + q * 512, 512), lhs, mvx,
                                                start=(ch == 0),
                                                stop=(ch == HCH - 1))
                    return inst
                t_mmgu = S.emit(
                    "tensor", gu_mm,
                    waits=[lastq if slot == 0 else None,
                           t_code[2 * HCH - 1] if slot == 0 else None,
                           t_gd.get((slot - 2, 1))], inc="pe")

                def gseq_act(sc, ic=ic, PSG=PSG, spar=spar):
                    inst = None
                    for q in range(2):
                        inst = sc.activation(
                            out=sg_sb[:, (spar * 2 + q) * 512:
                                      (spar * 2 + q) * 512 + 512],
                            in_=ps(PSG + q * 512, 512), func=AF.Silu,
                            scale=oscl[:, ic:ic + 1])
                    return inst
                t_ga = S.emit("scalar", gseq_act, waits=[t_mmgu], inc="ae")

                for q in range(2):
                    kidx = slot * 2 + q

                    def gq_dve(v, ic=ic, q=q, spar=spar, PSU=PSU, kidx=kidx):
                        if kidx >= 2:
                            v.tensor_tensor(mxcol[:, 0:1], mxcol[:, 0:1],
                                            mxcol[:, 1 + kidx % 2:2 + kidx % 2],
                                            ALU.max)
                        sg = sg_sb[:, (spar * 2 + q) * 512:
                                   (spar * 2 + q) * 512 + 512]
                        gt = scratch8[:, gtile_base + (kidx % 3) * 512:
                                      gtile_base + (kidx % 3) * 512 + 512]
                        v.tensor_scalar(gt, ps(PSU + q * 512, 512),
                                        oscl[:, ICH + ic:ICH + ic + 1], None,
                                        ALU.mult)
                        v.tensor_tensor(gt, gt, sg, ALU.mult)
                        return v.reduce_max(mxcol[:, 1 + kidx % 2:2 + kidx % 2],
                                            gt, axis=AX.X,
                                            apply_absolute_value=True)
                    t_gd[(slot, q)] = S.emit(
                        "vector", gq_dve,
                        waits=[t_ga if q == 0 else None,
                               t_gst.get((slot - 2, 1) if q == 0 else
                                         (slot - 1, 0))],
                        inc="ve")
                    tt = th * 2 + q
                    t_gst[(slot, q)] = S.dma(
                        "sync", bass.AP(gout, ic * 128 * TOK + tt * 512,
                                        [[TOK, 128], [1, 512]]),
                        scratch8[:, gtile_base + (kidx % 3) * 512:
                                 gtile_base + (kidx % 3) * 512 + 512],
                        waits=[t_gd[(slot, q)]],
                        inc="gs0" if slot % 2 == 0 else "gs1")
                slot += 1

        def fin_acc4(v):
            for kl in (1, 2):
                inst = v.tensor_tensor(mxcol[:, 0:1], mxcol[:, 0:1],
                                       mxcol[:, kl:kl + 1], ALU.max)
                inst.then_inc(sems["vq"], 1)
                S.vq_base += 1
                v.wait_ge(sems["vq"], S.vq_base)
            return v.tensor_copy(mxcol[:, 3:4], mxcol[:, 0:1])
        t_fa4 = S.emit("vector", fin_acc4, waits=[t_gd[(slot - 1, 1)]], inc="ve")
        t_m1 = S.dma("gpsimd", bass.AP(mxs, 0, [[1, 128]]),
                     bass.AP(mxcol, 0, [[4, 128]]),
                     waits=[t_fa4], inc="dm2")
        t_m2 = S.dma("gpsimd", mxrow[:, :], mxs.ap(), waits=[t_m1], inc="dm2")
        t_m3 = S.emit("vector",
                      lambda v: v.reduce_max(mx1[:, :], mxrow[:, :], axis=AX.X),
                      waits=[t_m2], inc="ve")
        t_m4 = S.dma("gpsimd", max2o.ap(), mx1[:, :], waits=[t_m3], inc="dm2")
        S.emit("gpsimd", lambda e: None,
               waits=[S.tok("gs0"), S.tok("gs1"), S.tok("dm2")])

        with nc.Block() as blk:
            S.replay(blk)

    return nc


def host_prep_l4(x1_by_core, inputs, core, mod_by_core, max1_by_core):
    m = core
    x1c = np.asarray(x1_by_core[m])
    x1t = np.ascontiguousarray(
        x1c.reshape(HCH, 128, NTT, 512).transpose(2, 1, 0, 3)
        .reshape(NTT * 128, HCH * 512))
    return {
        "x1": x1t,
        "mod_in": np.asarray(mod_by_core[m]).astype(np.float32),
        "max1all": np.concatenate([np.asarray(v).reshape(1, 1)
                                   for v in max1_by_core], axis=1).astype(np.float32),
        "clip_x": np.asarray(inputs["L0_ffn_clip_x"]).reshape(1, 1).astype(np.float32),
        "w_gT": np.ascontiguousarray(np.asarray(inputs["L0_ffn_w_gate"]).T).astype(np.float32),
        "w_uT": np.ascontiguousarray(np.asarray(inputs["L0_ffn_w_up"]).T).astype(np.float32),
        "w_g": np.ascontiguousarray(np.asarray(inputs["L0_ffn_w_gate"])).astype(np.float32),
        "w_u": np.ascontiguousarray(np.asarray(inputs["L0_ffn_w_up"])).astype(np.float32),
    }


L5_SEMN = ["dx1", "dgl", "dmo", "dmx", "dsc", "dwr", "dcl", "dws0", "dws1",
           "dwt0", "dwt1", "dos", "pe", "ve", "ae", "vq"]


def build_l5():
    nc = bass.Bass("TRN2", target_bir_lowering=False, debug=False,
                   detect_race_conditions=False)
    g = nc.dram_tensor("g", [II, TOK], BF16, kind="ExternalInput")
    x1 = nc.dram_tensor("x1", [D, TOK], BF16, kind="ExternalInput")
    mod_in = nc.dram_tensor("mod_in", [128, 96], F32, kind="ExternalInput")
    max2all = nc.dram_tensor("max2all", [1, 8], F32, kind="ExternalInput")
    clip_d = nc.dram_tensor("clip_d", [1, 1], F32, kind="ExternalInput")
    w_d = nc.dram_tensor("w_d", [D, II], F32, kind="ExternalInput")
    w_dT = nc.dram_tensor("w_dT", [II, D], F32, kind="ExternalInput")
    outt = nc.dram_tensor("outt", [D, TOK], F32, kind="ExternalOutput")
    scd = nc.dram_tensor("scd5", [1, 4], F32, kind="Internal")
    wrd = nc.dram_tensor("wrd5", [32, 128], F32, kind="Internal")

    ctx = contextlib.ExitStack()
    with ctx:
        sb = nc.sbuf_tensor
        codes_d = ctx.enter_context(sb("codes_d", [128, ICH * D], BF16))
        gsrc = ctx.enter_context(sb("gsrc", [128, 2 * ICH * 512], BF16))
        x1t = ctx.enter_context(sb("x1t5", [128, HCH * 512], BF16))
        out_sb = ctx.enter_context(sb("out_sb", [128, HCH * 512], F32))
        wt_db = ctx.enter_context(sb("wt_db5", [128, 2 * D], F32))
        ws_db = ctx.enter_context(sb("ws_db5", [128, 2 * II], F32))
        dsc_raw = ctx.enter_context(sb("dsc_raw", [128, 32], F32))
        drec = ctx.enter_context(sb("drec", [128, 32], F32))
        dscl16 = ctx.enter_context(sb("dscl16", [128, 16], F32))
        osclgb = ctx.enter_context(sb("osclgb", [128, 16], F32))
        wrowT = ctx.enter_context(sb("wrowT5", [32, 128], F32))
        winv_bcd = ctx.enter_context(sb("winv_bcd", [128, D], BF16))
        mod_sb = ctx.enter_context(sb("mod_sb5", [128, 96], F32))
        sc_row = ctx.enter_context(sb("sc_row5", [1, 8], F32))
        sc_one = ctx.enter_context(sb("sc_one5", [1, 4], F32))
        s2bc = ctx.enter_context(sb("s2bc", [128, 2], F32))
        tmpq = ctx.enter_context(sb("tmpq", [128, 512], F32))
        f16q = ctx.enter_context(sb("f16q", [128, 512], F16))
        arena = ctx.enter_context(nc.psum_tensor("arena5", [128, 4096], F32))

        def ps(off, n):
            return bass.AP(arena, off, [[4096, 128], [1, n]])

        sems = _sems(ctx, nc, L5_SEMN)
        S = Sched(nc, sems)

        def init_consts(v):
            v.memset(dsc_raw[:, :], 1.0)
            return v.memset(drec[:, :], 1.0)
        t_init = S.emit("vector", init_consts, inc="ve")

        t_mo = S.dma("sync", mod_sb[:, :], mod_in.ap(), inc="dmo")
        t_mxa = S.dma("sync", sc_row[:, :], max2all.ap(), inc="dmx")
        t_cl = S.dma("sync", sc_one[:, 3:4], clip_d.ap(), inc="dcl")

        t_gm = S.emit("vector",
                      lambda v: v.reduce_max(sc_one[:, 0:1], sc_row[:, :],
                                             axis=AX.X),
                      waits=[t_mxa], inc="ve")
        t_sg = S.emit("scalar",
                      lambda sc: sc.activation(out=sc_one[:, 2:3],
                                               in_=sc_one[:, 3:4],
                                               func=AF.Sigmoid),
                      waits=[t_cl], inc="ae")

        def s2fin(v):
            cnt = [0]

            def stp(inst):
                cnt[0] += 1
                inst.then_inc(sems["vq"], 1)
                v.wait_ge(sems["vq"], S.vq_base + cnt[0])
            stp(v.tensor_scalar(sc_one[:, 0:1], sc_one[:, 0:1], sc_one[:, 2:3],
                                1.0 / QMAX, ALU.mult, ALU.mult))
            stp(v.reciprocal(sc_one[:, 1:2], sc_one[:, 0:1]))
            S.vq_base += cnt[0]
            return v.tensor_copy(sc_one[:, 2:3], sc_one[:, 1:2])
        t_s2 = S.emit("vector", s2fin, waits=[t_gm, t_sg], inc="ve")
        t_sd = S.dma("gpsimd", scd.ap(), sc_one[:, 0:4], waits=[t_s2], inc="dsc")
        t_s2b = S.dma("gpsimd", s2bc[:, :],
                      bass.AP(scd, 0, [[0, 128], [1, 2]]), waits=[t_sd], inc="dsc")

        t_dsc = {}
        for hc in range(HCH):
            par = hc % 2
            t_ld = S.dma("sync", ws_db[:, par * II:(par + 1) * II],
                         bass.AP(w_d, hc * 128 * II, [[II, 128], [1, II]]),
                         waits=[t_dsc.get(hc - 2)],
                         inc="dws0" if par == 0 else "dws1")
            t_dsc[hc] = S.emit(
                "vector",
                lambda v, hc=hc, par=par: v.reduce_max(
                    dsc_raw[:, hc:hc + 1], ws_db[:, par * II:(par + 1) * II],
                    axis=AX.X, apply_absolute_value=True),
                waits=[t_ld], inc="ve")

        def dtail(v):
            cnt = [0]

            def stp(inst):
                cnt[0] += 1
                inst.then_inc(sems["vq"], 1)
                v.wait_ge(sems["vq"], S.vq_base + cnt[0])
            stp(v.tensor_scalar(drec[:, 16:24], dsc_raw[:, 0:8], s2bc[:, 0:1],
                                1.0 / QMAX, ALU.mult, ALU.mult))
            stp(v.tensor_copy(
                bass.AP(dscl16, 0, [[16, 128], [2, 8]]), drec[:, 16:24]))
            stp(v.tensor_copy(
                bass.AP(dscl16, 1, [[16, 128], [2, 8]]), drec[:, 16:24]))
            stp(v.tensor_tensor(osclgb[:, :], dscl16[:, :], mod_sb[:, 80:96],
                                ALU.mult))
            stp(v.reciprocal(drec[:, 0:8], dsc_raw[:, 0:8]))
            stp(v.tensor_single_scalar(drec[:, 0:8], drec[:, 0:8], QMAX,
                                       ALU.mult))
            S.vq_base += cnt[0]
            inst = None
            for pb in range(4):
                inst = v.transpose(wrowT[0:32, pb * 32:(pb + 1) * 32],
                                   drec[pb * 32:(pb + 1) * 32, 0:32])
            return inst
        t_dt = S.emit("vector", dtail,
                      waits=[t_dsc[HCH - 1], t_s2b, t_mo], inc="ve")
        t_wrd = S.dma("gpsimd", bass.AP(wrd, 0, [[128, 32], [1, 128]]),
                      bass.AP(wrowT, 0, [[128, 32], [1, 128]]),
                      waits=[t_dt], inc="dwr")
        t_bcd = S.dma("gpsimd", winv_bcd[:, :],
                      bass.AP(wrd, 0, [[0, 128], [128, HCH], [1, 128]]),
                      waits=[t_wrd], inc="dwr")

        t_code = {}
        for ic in range(ICH):
            par = ic % 2
            t_wt = S.dma("sync", wt_db[:, par * D:(par + 1) * D],
                         bass.AP(w_dT, ic * 128 * D, [[D, 128], [1, D]]),
                         waits=[t_code.get(ic - 2)],
                         inc="dwt0" if par == 0 else "dwt1")

            def code_fn(v, ic=ic, par=par):
                dst = codes_d[:, ic * D:(ic + 1) * D]
                v.tensor_tensor(dst, wt_db[:, par * D:(par + 1) * D],
                                winv_bcd[:, :], ALU.mult)
                v.tensor_single_scalar(f16q[:, :], dst[:, 0:512], MAGIC, ALU.add)
                v.tensor_single_scalar(dst[:, 0:512], f16q[:, :], MAGIC,
                                       ALU.subtract)
                v.tensor_single_scalar(f16q[:, :], dst[:, 512:1024], MAGIC,
                                       ALU.add)
                return v.tensor_single_scalar(dst[:, 512:1024], f16q[:, :],
                                              MAGIC, ALU.subtract)
            t_code[ic] = S.emit("vector", code_fn,
                                waits=[t_wt, t_bcd if ic == 0 else None],
                                inc="ve")

        t_gl = {}
        t_hq = {}
        t_mm = {}
        t_ev = {}
        t_os = {}

        def emit_gl(tt):
            gslot = (tt % 2) * ICH * 512
            t_gl[tt] = S.dma(
                "sync", bass.AP(gsrc, gslot,
                                [[2 * ICH * 512, 128], [512, ICH], [1, 512]]),
                bass.AP(g, tt * 512, [[TOK, 128], [128 * TOK, ICH], [1, 512]]),
                waits=[t_mm.get((tt - 2, HCH - 1))], inc="dgl")

        def emit_hq(tt, ic):
            gslot = (tt % 2) * ICH * 512

            def hq_fn(v, ic=ic, gslot=gslot):
                sl = gsrc[:, gslot + ic * 512:gslot + (ic + 1) * 512]
                v.tensor_scalar(tmpq[:, :], sl, s2bc[:, 1:2], CLIP_LO,
                                ALU.mult, ALU.max)
                v.tensor_single_scalar(tmpq[:, :], tmpq[:, :], CLIP_HI, ALU.min)
                v.tensor_single_scalar(f16q[:, :], tmpq[:, :], MAGIC, ALU.add)
                return v.tensor_single_scalar(sl, f16q[:, :], MAGIC,
                                              ALU.subtract)
            t_hq[(tt, ic)] = S.emit(
                "vector", hq_fn,
                waits=[t_gl[tt] if ic == 0 else None,
                       (t_s2b if (tt, ic) == (0, 0) else None)],
                inc="ve")

        for tt in range(NTT):
            gslot = (tt % 2) * ICH * 512
            if tt == 0:
                emit_gl(0)
            if tt + 1 < NTT:
                emit_gl(tt + 1)
            t_x1 = S.dma(
                "sync", bass.AP(x1t, 0, [[HCH * 512, 128], [512, HCH], [1, 512]]),
                bass.AP(x1, tt * 512, [[TOK, 128], [128 * TOK, HCH], [1, 512]]),
                waits=[t_ev.get((tt - 1, HCH - 1))], inc="dx1")
            if tt == 0:
                for ic in range(ICH):
                    emit_hq(0, ic)
            for hc in range(HCH):
                po = ((tt * HCH + hc) % 2) * 512

                def dmm(t, hc=hc, po=po, gslot=gslot):
                    inst = None
                    for ic in range(ICH):
                        inst = t.matmul(ps(po, 512),
                                        codes_d[:, ic * D + hc * 128:
                                                ic * D + hc * 128 + 128],
                                        gsrc[:, gslot + ic * 512:
                                             gslot + (ic + 1) * 512],
                                        start=(ic == 0), stop=(ic == ICH - 1))
                    return inst
                t_mm[(tt, hc)] = S.emit(
                    "tensor", dmm,
                    waits=[t_hq[(tt, ICH - 1)] if hc == 0 else None,
                           t_code[ICH - 1] if (tt, hc) == (0, 0) else None,
                           t_ev.get(((tt * HCH + hc - 2) // HCH,
                                     (tt * HCH + hc - 2) % HCH))],
                    inc="pe")

                def dev_fn(v, tt=tt, hc=hc, po=po):
                    bq = 0 if tt < NTT // 2 else 1
                    col = hc * 2 + bq
                    v.tensor_scalar(tmpq[:, :], ps(po, 512),
                                    osclgb[:, col:col + 1], None, ALU.mult)
                    return v.tensor_tensor(
                        out_sb[:, hc * 512:(hc + 1) * 512], tmpq[:, :],
                        x1t[:, hc * 512:(hc + 1) * 512], ALU.add)
                t_ev[(tt, hc)] = S.emit(
                    "vector", dev_fn,
                    waits=[t_mm[(tt, hc)], t_x1 if hc == 0 else None,
                           t_dt if (tt, hc) == (0, 0) else None,
                           (t_os.get(tt - 1) if hc == 0 else None)],
                    inc="ve")
                if tt + 1 < NTT:
                    for ic in range(3 * hc, min(ICH, 3 * hc + 3)):
                        emit_hq(tt + 1, ic)
            t_os[tt] = S.dma(
                "gpsimd", bass.AP(outt, tt * 512,
                                  [[TOK, 128], [128 * TOK, HCH], [1, 512]]),
                bass.AP(out_sb, 0, [[HCH * 512, 128], [512, HCH], [1, 512]]),
                waits=[t_ev[(tt, HCH - 1)]], inc="dos")

        S.emit("gpsimd", lambda e: None, waits=[S.tok("dos")])

        with nc.Block() as blk:
            S.replay(blk)

    return nc


def host_prep_l5(g_by_core, x1_by_core, inputs, core, mod_by_core, max2_by_core):
    m = core
    return {
        "g": np.asarray(g_by_core[m]),
        "x1": np.asarray(x1_by_core[m]),
        "mod_in": np.asarray(mod_by_core[m]).astype(np.float32),
        "max2all": np.concatenate([np.asarray(v).reshape(1, 1)
                                   for v in max2_by_core], axis=1).astype(np.float32),
        "clip_d": np.asarray(inputs["L0_ffn_clip_down"]).reshape(1, 1).astype(np.float32),
        "w_d": np.ascontiguousarray(np.asarray(inputs["L0_ffn_w_down"])).astype(np.float32),
        "w_dT": np.ascontiguousarray(np.asarray(inputs["L0_ffn_w_down"]).T).astype(np.float32),
    }


def host_prep_l1(inputs, core):
    b0 = core * BPC
    xs = np.asarray(inputs["L0_attn_x"][b0:b0 + BPC])
    xr = xs.reshape(BPC, NJ, 128, D)[:, :, ::-1, :].reshape(BPC, T, D)
    xrev = xr.transpose(2, 0, 1).reshape(D, TOK)
    xrev = np.ascontiguousarray(
        xrev.reshape(HCH, 128, NTT, 512).transpose(2, 1, 0, 3)
        .reshape(NTT * 128, HCH * 512))
    cts = np.ascontiguousarray(np.asarray(inputs["L0_c"][b0:b0 + BPC]).T)
    return {
        "xrev": xrev.astype(np.float32),
        "cts": cts.astype(np.float32),
        "wmod": np.ascontiguousarray(inputs["L0_wmod"]).astype(np.float32),
        "w_inT": np.ascontiguousarray(np.asarray(inputs["L0_attn_w_in"]).T).astype(np.float32),
        "b_in": np.ascontiguousarray(np.asarray(inputs["L0_attn_b_in"])[None, :]).astype(np.float32),
    }


def kernel(**inputs):
    cores = list(range(NCORES))
    r1 = run_bass_kernel_spmd(build_l1(), [host_prep_l1(inputs, c) for c in cores],
                              core_ids=cores)
    z_by = [np.asarray(r1.results[c]["zout"]) for c in cores]
    mod_by = [np.asarray(r1.results[c]["mod_out"]) for c in cores]

    r2 = run_bass_kernel_spmd(build_l2(), [host_prep_l2(z_by, inputs, c)
                                           for c in cores], core_ids=cores)
    gated_by = [np.asarray(r2.results[c]["gated"]) for c in cores]

    r3 = run_bass_kernel_spmd(build_l3(), [host_prep_l3(gated_by, inputs, c, mod_by)
                                           for c in cores], core_ids=cores)
    x1_by = [np.asarray(r3.results[c]["x1o"]) for c in cores]
    max1_by = [np.asarray(r3.results[c]["max1o"]) for c in cores]

    r4 = run_bass_kernel_spmd(build_l4(), [host_prep_l4(x1_by, inputs, c, mod_by,
                                                        max1_by) for c in cores],
                              core_ids=cores)
    g_by = [np.asarray(r4.results[c]["gout"]) for c in cores]
    max2_by = [np.asarray(r4.results[c]["max2o"]) for c in cores]

    r5 = run_bass_kernel_spmd(build_l5(), [host_prep_l5(g_by, x1_by, inputs, c,
                                                        mod_by, max2_by)
                                           for c in cores], core_ids=cores)
    out = np.empty((NCORES * BPC, T, D), np.float32)
    for c in cores:
        o = np.asarray(r5.results[c]["outt"])
        out[2 * c] = o[:, :T].T
        out[2 * c + 1] = o[:, T:].T
    return out


# version 09 (final)
# speedup vs baseline: 1.3567x; 1.0104x over previous
import contextlib
import numpy as np
import ml_dtypes
import concourse.bass as bass
import concourse.mybir as mybir
from concourse.bass_utils import run_bass_kernel_spmd

F32 = mybir.dt.float32
BF16 = mybir.dt.bfloat16
F16 = mybir.dt.float16
AF = mybir.ActivationFunctionType
ALU = mybir.AluOpType
AX = mybir.AxisListType
BFNP = ml_dtypes.bfloat16

NCORES = 8
BPC = 2
T = 2048
TOK = BPC * T
D = 1024
E = 2048
II = 2816
HCH = D // 128
ECH = E // 128
ICH = II // 128
NTT = TOK // 512
NJ = T // 128
EPS = 1e-5
CLIP_LO = -128.49
CLIP_HI = 127.49
MAGIC = 1536.0
QMAX = 127.0

ENGINES = ("sync", "gpsimd", "scalar", "vector", "tensor")


class Item:
    __slots__ = ("fn", "waits", "inc")

    def __init__(self, fn, waits, inc):
        self.fn = fn
        self.waits = waits
        self.inc = inc


class Sched:

    def __init__(self, nc, sems):
        self.nc = nc
        self.sems = sems
        self.items = {e: [] for e in ENGINES}
        self.val = {n: 0 for n in sems}
        self.vq_base = 0

    def emit(self, eng, fn, waits=(), inc=None, inc_n=1):
        tok = None
        if inc is not None:
            self.val[inc] += inc_n
            tok = (inc, self.val[inc], inc_n)
        self.items[eng].append(Item(fn, [w for w in waits if w is not None], tok))
        return tok

    def dma(self, eng, out, in_, waits=(), inc="dma", **kw):
        def fn(e):
            return e.dma_start(out=out, in_=in_, **kw)
        return self.emit(eng, fn, waits=waits, inc=inc, inc_n=16)

    def tok(self, sem):
        return (sem, self.val[sem], 0) if self.val[sem] else None

    def replay(self, blk):
        for eng in ENGINES:
            items = self.items[eng]
            if not items:
                continue

            def make(eng_items, ename):
                def body(e):
                    for it in eng_items:
                        for (sem, v, _s) in it.waits:
                            e.wait_ge(self.sems[sem], v)
                        inst = it.fn(e)
                        if it.inc is not None:
                            if inst is None:
                                inst = e.nop()
                            inst.then_inc(self.sems[it.inc[0]], it.inc[2])
                return body
            getattr(blk, eng)(make(items, eng))


def _sems(ctx, nc, names):
    return {n: ctx.enter_context(nc.semaphore(n)) for n in names}


SEMN = ["dct", "dwi", "xl0", "xl1", "wm0", "wm1", "dbb",
        "zst0", "zst1", "pe", "ve", "ae", "st", "vq"]


def build_l1():
    nc = bass.Bass("TRN2", target_bir_lowering=False, debug=False,
                   detect_race_conditions=False, dynamic_dma_scratch_size=8192)
    xrev = nc.dram_tensor("xrev", [NTT * 128, HCH * 512], F32,
                          kind="ExternalInput")
    cts = nc.dram_tensor("cts", [D, BPC], F32, kind="ExternalInput")
    wmod = nc.dram_tensor("wmod", [D, 6 * D], F32, kind="ExternalInput")
    w_inT = nc.dram_tensor("w_inT", [D, E], F32, kind="ExternalInput")
    b_in = nc.dram_tensor("b_in", [1, E], F32, kind="ExternalInput")
    zout = nc.dram_tensor("zout", [TOK, E], BF16, kind="ExternalOutput")
    mod_out = nc.dram_tensor("mod_out", [128, 96], F32, kind="ExternalOutput")
    zcd = nc.dram_tensor("zcd1", [1, 2 * E], F32, kind="Internal")

    ctx = contextlib.ExitStack()
    with ctx:
        sb = nc.sbuf_tensor
        wmod_db = ctx.enter_context(sb("wmod_db", [128, 2 * 6 * D], F32))
        ct_sb = ctx.enter_context(sb("ct_sb", [128, HCH * BPC], F32))
        silu_c = ctx.enter_context(sb("silu_c", [128, HCH * BPC], F32))
        mod_sb = ctx.enter_context(sb("mod_sb", [128, 96], F32))
        mtmp = ctx.enter_context(sb("mtmp", [128, 32], F32))
        sa_bf = ctx.enter_context(sb("sa_bf", [128, 16], BF16))
        ones = ctx.enter_context(sb("ones", [128, 128], BF16))
        x_db = ctx.enter_context(sb("x_db", [128, 2 * HCH * 512], BF16))
        xm_db = ctx.enter_context(sb("xm_db", [128, 2 * HCH * 512], BF16))
        xsq = ctx.enter_context(sb("xsq", [128, HCH * 512], BF16))
        stat = ctx.enter_context(sb("stat", [128, 7 * 512], F32))
        w_in_sb = ctx.enter_context(sb("w_in_sb", [128, HCH * E], BF16))
        binbc = ctx.enter_context(sb("binbc", [128, E], F32))
        zcb = ctx.enter_context(sb("zcb", [128, 2 * E], F32))
        zc_row = ctx.enter_context(sb("zc_row", [1, 2 * E], F32))
        rstdT = ctx.enter_context(sb("rstdT", [128, 2 * 128], F32))
        z_db = ctx.enter_context(sb("z_db", [128, 2 * E], BF16))
        eps_sb = ctx.enter_context(sb("eps_sb", [128, 1], F32))

        arena = ctx.enter_context(nc.psum_tensor("arena", [128, 4096], F32))

        def ps(off, n):
            return bass.AP(arena, off, [[4096, 128], [1, n]])

        def pz(off, n):
            return bass.AP(arena, off, [[4096, 1], [1, n]])

        PS_STAT = 0
        PS_MOD = 2048
        PS_ZA = 1024
        PS_ZB = 2048
        PS_ZC = 3072

        sems = _sems(ctx, nc, SEMN)
        S = Sched(nc, sems)

        SM, SQ, SVAR, SRSTD, SU0 = (i * 512 for i in range(5))
        SSTD = SQ

        t_ct = S.dma("gpsimd", ct_sb[:, :],
                     bass.AP(cts, 0, [[2, 128], [256, 8], [1, 2]]), inc="dct")
        t_xl = {}
        for tt in range(2):
            t_xl[tt] = S.dma(
                "gpsimd", x_db[:, (tt % 2) * HCH * 512:
                               (tt % 2 + 1) * HCH * 512],
                bass.AP(xrev, tt * 128 * HCH * 512,
                        [[HCH * 512, 128], [1, HCH * 512]]),
                inc="xl0" if tt % 2 == 0 else "xl1")
        t_silu = S.emit("scalar",
                        lambda sc: sc.activation(out=silu_c[:, :], in_=ct_sb[:, :],
                                                 func=AF.Silu),
                        waits=[t_ct], inc="ae")

        def init_consts(v):
            v.memset(ones[:, :], 1.0)
            return v.memset(eps_sb[:, :], EPS)
        t_ones = S.emit("vector", init_consts, inc="ve")

        t_modmm = {}
        t_macc = {}
        t_wm = {}
        for ch in range(HCH):
            half = (ch % 2) * 6 * D
            pmod = PS_MOD + (ch % 2) * 512
            t_wm[ch] = S.dma("sync", wmod_db[:, half:half + 6 * D],
                             bass.AP(wmod, ch * 128 * 6 * D,
                                     [[6 * D, 128], [1, 6 * D]]),
                             waits=[t_modmm.get(ch - 2)],
                             inc="wm0" if ch % 2 == 0 else "wm1")

            def mod_mm(t, ch=ch, half=half, pmod=pmod):
                inst = None
                for ic in range(48):
                    inst = t.matmul(ps(pmod + 2 * ic, 2),
                                    wmod_db[:, half + ic * 128:half + (ic + 1) * 128],
                                    silu_c[:, 2 * ch:2 * ch + 2],
                                    start=True, stop=True)
                return inst
            t_modmm[ch] = S.emit("tensor", mod_mm,
                                 waits=[t_silu if ch == 0 else None, t_wm[ch],
                                        t_macc.get(ch - 2)],
                                 inc="pe")

            def mod_acc(v, ch=ch, pmod=pmod):
                if ch == 0:
                    return v.tensor_copy(mod_sb[:, :], ps(pmod, 96))
                return v.tensor_tensor(mod_sb[:, :], mod_sb[:, :], ps(pmod, 96),
                                       ALU.add)
            t_macc[ch] = S.emit("vector", mod_acc, waits=[t_modmm[ch]], inc="ve")

        t_sabf = S.emit("vector",
                        lambda v: v.tensor_copy(sa_bf[:, :], mod_sb[:, 0:16]),
                        waits=[t_macc[HCH - 1]], inc="ve")

        def mod_post(v):
            cnt = [0]

            def step(inst):
                cnt[0] += 1
                inst.then_inc(sems["vq"], 1)
                v.wait_ge(sems["vq"], S.vq_base + cnt[0])
            for base in (32, 80):
                sl = mod_sb[:, base:base + 16]
                step(v.tensor_single_scalar(mtmp[:, 0:16], sl, 0.0, ALU.is_ge))
                step(v.tensor_scalar(mtmp[:, 0:16], mtmp[:, 0:16], 2.0, -1.0,
                                     ALU.mult, ALU.add))
                step(v.tensor_scalar(mtmp[:, 16:32], sl, -1.0, None, ALU.mult))
                step(v.tensor_tensor(mtmp[:, 16:32], sl, mtmp[:, 16:32], ALU.max))
                step(v.tensor_single_scalar(mtmp[:, 16:32], mtmp[:, 16:32], 0.25,
                                            ALU.max))
                step(v.tensor_tensor(sl, mtmp[:, 0:16], mtmp[:, 16:32], ALU.mult))
            S.vq_base += cnt[0]
            inst = None
            for base in (16, 64):
                inst = v.tensor_single_scalar(mod_sb[:, base:base + 16],
                                              mod_sb[:, base:base + 16], 1.0, ALU.add)
            return inst
        t_modpost = S.emit("vector", mod_post, waits=[t_macc[HCH - 1]], inc="ve")
        t_modst = S.dma("scalar", mod_out.ap(), mod_sb[:, :], waits=[t_modpost],
                        inc="st")

        t_win = {}
        for ch in range(HCH):
            t_win[ch] = S.dma("gpsimd", w_in_sb[:, ch * E:(ch + 1) * E],
                              bass.AP(w_inT, ch * 128 * E, [[E, 128], [1, E]]),
                              inc="dwi")
        t_bin = S.dma("sync", binbc[:, :], bass.AP(b_in, 0, [[0, 128], [1, E]]),
                      inc="dbb")

        t_zc = {}
        t_zcev = {}
        for k in range(4):
            b, half2 = divmod(k, 2)

            def zc_mm(t, b=b, half2=half2):
                inst = None
                for eq in range(2):
                    for ch in range(HCH):
                        inst = t.matmul(
                            pz(PS_ZC + eq * 512, 512),
                            sa_bf[:, ch * 2 + b:ch * 2 + b + 1],
                            w_in_sb[:, ch * E + (half2 * 2 + eq) * 512:
                                    ch * E + (half2 * 2 + eq) * 512 + 512],
                            start=(ch == 0), stop=(ch == HCH - 1))
                return inst
            t_zc[k] = S.emit("tensor", zc_mm,
                             waits=[t_sabf if k == 0 else None,
                                    t_win[HCH - 1] if k == 0 else None,
                                    t_zcev.get(k - 1)], inc="pe")
            t_zcev[k] = S.emit(
                "vector",
                lambda v, b=b, half2=half2: v.tensor_copy(
                    zc_row[0:1, b * E + half2 * 1024:b * E + half2 * 1024 + 1024],
                    pz(PS_ZC, 1024)),
                waits=[t_zc[k]], inc="ve")
        t_zcs = S.dma("sync", zcd.ap(), zc_row[0:1, :], waits=[t_zcev[3]],
                      inc="dbb")
        t_zcb = {}
        for b in range(2):
            t_zcb[b] = S.dma("sync", zcb[:, b * E:(b + 1) * E],
                             bass.AP(zcd, b * E, [[0, 128], [1, E]]),
                             waits=[t_zcs if b == 0 else None], inc="dbb")

        def zcb_add(v):
            inst = None
            for b in range(2):
                inst = v.tensor_tensor(zcb[:, b * E:(b + 1) * E],
                                       zcb[:, b * E:(b + 1) * E],
                                       binbc[:, :], ALU.add)
            return inst
        t_zcball = S.emit("vector", zcb_add, waits=[t_zcb[1], t_bin], inc="ve")

        t_xm = {}
        t_pe_stats = {}
        t_mq = {}
        t_std = {}
        t_rstr = {}
        t_zdma = {}
        t_zeva = {}
        t_zevb = {}
        t_mmb_tile = {}
        for tt in range(NTT):
            xs = (tt % 2) * HCH * 512
            if tt >= 2:
                t_xl[tt] = S.dma(
                    "gpsimd", x_db[:, xs:xs + HCH * 512],
                    bass.AP(xrev, tt * 128 * HCH * 512,
                            [[HCH * 512, 128], [1, HCH * 512]]),
                    waits=[t_xm.get((tt - 2, HCH - 1))],
                    inc="xl0" if tt % 2 == 0 else "xl1")
            t_sq = S.emit("scalar",
                          lambda sc, xs=xs: sc.activation(
                              out=xsq[:, :], in_=x_db[:, xs:xs + HCH * 512],
                              func=AF.Square),
                          waits=[t_xl[tt], t_pe_stats.get(tt - 1)], inc="ae")

            def stats_mm(t, xs=xs):
                inst = None
                for ch in range(HCH):
                    t.matmul(ps(PS_STAT, 512),
                             ones[:, :], x_db[:, xs + ch * 512: xs + ch * 512 + 512],
                             start=(ch == 0), stop=(ch == HCH - 1))
                for ch in range(HCH):
                    inst = t.matmul(ps(PS_STAT + 512, 512),
                                    ones[:, :], xsq[:, ch * 512:(ch + 1) * 512],
                                    start=(ch == 0), stop=(ch == HCH - 1))
                return inst
            t_pe_stats[tt] = S.emit("tensor", stats_mm,
                                    waits=[t_ones if tt == 0 else None, t_sq,
                                           t_mq.get(tt - 1)], inc="pe")

            def mq(v):
                v.tensor_scalar(stat[:, SM:SM + 512], ps(PS_STAT, 512),
                                1.0 / D, None, ALU.mult)
                v.tensor_scalar(stat[:, SQ:SQ + 512], ps(PS_STAT + 512, 512),
                                1.0 / D, None, ALU.mult)
                v.tensor_tensor(stat[:, SVAR:SVAR + 512], stat[:, SM:SM + 512],
                                stat[:, SM:SM + 512], ALU.mult)
                return v.tensor_tensor(stat[:, SVAR:SVAR + 512], stat[:, SQ:SQ + 512],
                                       stat[:, SVAR:SVAR + 512], ALU.subtract)
            t_mq[tt] = S.emit("vector", mq,
                              waits=[t_pe_stats[tt], t_std.get(tt - 1)], inc="ve")
            t_std[tt] = S.emit("scalar",
                               lambda sc: sc.activation(out=stat[:, SSTD:SSTD + 512],
                                                        in_=stat[:, SVAR:SVAR + 512],
                                                        func=AF.Sqrt, bias=eps_sb[:, 0:1]),
                               waits=[t_mq[tt]], inc="ae")
            bq = 0 if tt < NTT // 2 else 1
            for ch in range(HCH):
                def xm_fn(v, xs=xs, ch=ch, bq=bq):
                    col = 16 + ch * 2 + bq
                    v.tensor_tensor(stat[:, SU0:SU0 + 512],
                                    x_db[:, xs + ch * 512: xs + ch * 512 + 512],
                                    stat[:, SM:SM + 512], ALU.subtract)
                    return v.tensor_scalar(
                        xm_db[:, xs + ch * 512: xs + ch * 512 + 512],
                        stat[:, SU0:SU0 + 512],
                        mod_sb[:, col:col + 1], None, ALU.mult)
                t_xm[(tt, ch)] = S.emit(
                    "vector", xm_fn,
                    waits=[t_mq[tt] if ch == 0 else None,
                           t_modpost if (tt, ch) == (0, 0) else None,
                           (t_mmb_tile.get(tt - 2) if ch == 0 else None)],
                    inc="ve")

            def rstr_fn(v, tt=tt):
                inst = v.reciprocal(stat[:, SRSTD:SRSTD + 512],
                                    stat[:, SSTD:SSTD + 512])
                inst.then_inc(sems["vq"], 1)
                S.vq_base += 1
                v.wait_ge(sems["vq"], S.vq_base)
                par = (tt % 2) * 128
                for j in range(4):
                    for k in range(4):
                        inst = v.transpose(
                            rstdT[k * 32:(k + 1) * 32,
                                  par + j * 32: par + j * 32 + 32],
                            stat[0:32, SRSTD + j * 128 + k * 32:
                                 SRSTD + j * 128 + (k + 1) * 32])
                return inst
            t_rstr[tt] = S.emit("vector", rstr_fn, waits=[t_std[tt]], inc="ve")

            for jq in range(4):
                bq2 = tt // 4
                jt = (tt % 4) * 4 + jq
                gidx = bq2 * NJ + jt
                zs = (gidx % 2) * E
                j4 = jt % 4

                def inproj_a(t, xs=xs, j4=j4):
                    inst = None
                    for ch in range(HCH):
                        lhs = xm_db[:, xs + ch * 512 + j4 * 128:
                                    xs + ch * 512 + j4 * 128 + 128]
                        for eq in range(2):
                            inst = t.matmul(ps(PS_ZA + eq * 512, 512), lhs,
                                            w_in_sb[:, ch * E + eq * 512:
                                                    ch * E + eq * 512 + 512],
                                            start=(ch == 0), stop=(ch == HCH - 1))
                    return inst
                t_mma = S.emit("tensor", inproj_a,
                               waits=[t_xm[(tt, HCH - 1)] if jq == 0 else None,
                                      t_zeva.get(gidx - 1)], inc="pe")

                def inproj_b(t, xs=xs, j4=j4):
                    inst = None
                    for ch in range(HCH):
                        lhs = xm_db[:, xs + ch * 512 + j4 * 128:
                                    xs + ch * 512 + j4 * 128 + 128]
                        for eq in range(2):
                            inst = t.matmul(ps(PS_ZB + eq * 512, 512), lhs,
                                            w_in_sb[:, ch * E + (2 + eq) * 512:
                                                    ch * E + (2 + eq) * 512 + 512],
                                            start=(ch == 0), stop=(ch == HCH - 1))
                    return inst
                t_mmb = S.emit("tensor", inproj_b,
                               waits=[t_macc[HCH - 1] if gidx == 0 else None,
                                      t_zevb.get(gidx - 1)], inc="pe")
                t_mmb_tile[tt] = t_mmb

                def zev_a(v, zs=zs, tt=tt, j4=j4, bq2=bq2):
                    par = (tt % 2) * 128
                    rcol = rstdT[:, par + j4 * 32: par + j4 * 32 + 1]
                    return v.scalar_tensor_tensor(
                        z_db[:, zs:zs + 1024], ps(PS_ZA, 1024), rcol,
                        zcb[:, bq2 * E: bq2 * E + 1024], ALU.mult, ALU.add)
                t_zeva[gidx] = S.emit(
                    "vector", zev_a,
                    waits=[t_mma, t_zcball if gidx == 0 else None,
                           t_zdma.get(gidx - 2)], inc="ve")

                def zev_b(v, zs=zs, tt=tt, j4=j4, bq2=bq2):
                    par = (tt % 2) * 128
                    rcol = rstdT[:, par + j4 * 32: par + j4 * 32 + 1]
                    return v.scalar_tensor_tensor(
                        z_db[:, zs + 1024:zs + 2048], ps(PS_ZB, 1024), rcol,
                        zcb[:, bq2 * E + 1024: bq2 * E + 2048], ALU.mult,
                        ALU.add)
                t_zevb[gidx] = S.emit("vector", zev_b, waits=[t_mmb], inc="ve")
                t_zdma[gidx] = S.dma(
                    "sync", bass.AP(zout, (bq2 * T + jt * 128) * E,
                                    [[E, 128], [1, E]]),
                    z_db[:, zs:zs + E], waits=[t_zevb[gidx]],
                    inc="zst0" if gidx % 2 == 0 else "zst1")

        S.emit("sync", lambda e: None,
               waits=[S.tok("zst0"), S.tok("zst1")])
        S.emit("scalar", lambda e: None, waits=[S.tok("st")])

        with nc.Block() as blk:
            S.replay(blk)

    return nc


L2_SEMN = ["dzi", "dbb", "tp0", "tp1", "gst0", "gst1", "pe", "ve", "ae"]
NPAIR = 128
GPAIRS = 8
NGRP = NPAIR // GPAIRS


def build_l2():
    nc = bass.Bass("TRN2", target_bir_lowering=False, debug=False,
                   detect_race_conditions=False)
    zin = nc.dram_tensor("zin", [16 * NJ * 128, 256], BF16, kind="ExternalInput")
    wrevcq = nc.dram_tensor("wrevcq", [768, 256], F32, kind="ExternalInput")
    b_ab = nc.dram_tensor("b_ab", [1, 256], F32, kind="ExternalInput")
    gated = nc.dram_tensor("gated", [256 * 128, 256], BF16, kind="ExternalOutput")

    ctx = contextlib.ExitStack()
    with ctx:
        sb = nc.sbuf_tensor
        zin_sb = ctx.enter_context(sb("zin_sb", [128, 16 * NJ * 256], BF16))
        toep_db = ctx.enter_context(sb("toep_db", [128, 2 * 6 * GPAIRS * 128], BF16))
        bb_bc = ctx.enter_context(sb("bb_bc", [128, 256], F32))
        sig_sb = ctx.enter_context(sb("sig_sb", [128, 4 * 256], BF16))
        lin_sb = ctx.enter_context(sb("lin_sb", [128, 4 * 256], BF16))
        gbuf = ctx.enter_context(sb("gbuf", [128, 2 * 2 * GPAIRS * 256], BF16))
        arena = ctx.enter_context(nc.psum_tensor("arena", [128, 4096], F32))

        def ps(off, n):
            return bass.AP(arena, off, [[4096, 128], [1, n]])

        sems = _sems(ctx, nc, L2_SEMN)
        S = Sched(nc, sems)

        t_zi = None
        for b in range(16):
            t_zi = S.dma(
                "sync", bass.AP(zin_sb, b * NJ * 256,
                                [[16 * NJ * 256, 128], [256, NJ], [1, 256]]),
                bass.AP(zin, b * NJ * 128 * 256,
                        [[256, 128], [128 * 256, NJ], [1, 256]]),
                inc="dzi")
        t_bb = S.dma("sync", bb_bc[:, :], bass.AP(b_ab, 0, [[0, 128], [1, 256]]),
                     inc="dbb")

        t_mm = {}
        t_dve = {}
        t_st = {}
        t_tp = {}
        for g in range(NGRP):
            par = g % 2
            tslot = par * 6 * GPAIRS * 128
            last_mm_war = t_mm.get((g - 2, GPAIRS - 1))
            tok = None
            for half, rbase in ((0, 3 * GPAIRS * g), (1, 384 + 3 * GPAIRS * g)):
                for piece in range(1):
                    tok = S.dma(
                        "gpsimd",
                        bass.AP(toep_db, tslot + half * 3 * GPAIRS * 128 + piece * 24 * 128,
                                [[2 * 6 * GPAIRS * 128, 128], [128, 24], [1, 128]]),
                        bass.AP(wrevcq, (rbase + piece * 24) * 256,
                                [[1, 128], [256, 24], [1, 128]]),
                        waits=[last_mm_war if (half, piece) == (0, 0) else None],
                        inc="tp0" if par == 0 else "tp1")
            t_tp[g] = tok
            for pg in range(GPAIRS):
                p = g * GPAIRS + pg
                pp = p % 2
                pa, pb = pp * 1024, pp * 1024 + 512
                sa_c, sb_c = (pp * 2) * 256, (pp * 2 + 1) * 256

                def conv_mm(t, pg=pg, p=p, tslot=tslot, pa=pa, pb=pb):
                    inst = None
                    for half, po in ((0, pa), (1, pb)):
                        blk0 = tslot + half * 3 * GPAIRS * 128 + (pg * 3) * 128
                        ccol = p + half * 128
                        for q in range(3):
                            mv = bass.AP(zin_sb, ccol,
                                         [[16 * NJ * 256, 128], [256, NJ - q],
                                          [NJ * 256, 16]])
                            pso = bass.AP(arena, po + q * 16,
                                          [[4096, 128], [1, (NJ - q) * 16]])
                            inst = t.matmul(pso,
                                            toep_db[:, blk0 + q * 128:
                                                    blk0 + (q + 1) * 128],
                                            mv, start=(q == 0), stop=(q == 2))
                    return inst
                t_mm[(g, pg)] = S.emit(
                    "tensor", conv_mm,
                    waits=[t_zi if p == 0 else None,
                           t_tp[g] if pg == 0 else None,
                           t_dve.get((g, pg - 2)) if pg >= 2 else
                           (t_dve.get((g - 1, GPAIRS - 2 + pg)) if g >= 1 else None)],
                    inc="pe")

                def gate_act(sc, p=p, pa=pa, pb=pb, sa_c=sa_c, sb_c=sb_c):
                    sc.activation(out=sig_sb[:, sa_c:sa_c + 256],
                                  in_=ps(pa, 256), func=AF.Sigmoid,
                                  bias=bb_bc[:, p:p + 1])
                    return sc.activation(out=sig_sb[:, sb_c:sb_c + 256],
                                         in_=ps(pb, 256), func=AF.Sigmoid,
                                         bias=bb_bc[:, 128 + p:128 + p + 1])
                t_act = S.emit("scalar", gate_act,
                               waits=[t_mm[(g, pg)], t_bb if p == 0 else None],
                               inc="ae")

                def gate_dve(v, g=g, pg=pg, p=p, pa=pa, pb=pb, par=par,
                             sa_c=sa_c, sb_c=sb_c):
                    base = par * 2 * GPAIRS * 256
                    va = base + pg * 256
                    vb = base + (GPAIRS + pg) * 256
                    la, lb = sa_c, sb_c
                    v.tensor_scalar_add(lin_sb[:, la:la + 256], ps(pa, 256),
                                        bb_bc[:, p:p + 1])
                    v.tensor_scalar_add(lin_sb[:, lb:lb + 256], ps(pb, 256),
                                        bb_bc[:, 128 + p:128 + p + 1])
                    v.tensor_tensor(gbuf[:, va:va + 256], lin_sb[:, la:la + 256],
                                    sig_sb[:, sb_c:sb_c + 256], ALU.mult)
                    return v.tensor_tensor(gbuf[:, vb:vb + 256],
                                           lin_sb[:, lb:lb + 256],
                                           sig_sb[:, sa_c:sa_c + 256], ALU.mult)
                t_dve[(g, pg)] = S.emit(
                    "vector", gate_dve,
                    waits=[t_act, (t_st.get(g - 2) if pg == 0 else None)],
                    inc="ve")
            for half in range(2):
                t_st[g] = S.dma(
                    "sync",
                    bass.AP(gated, ((half * 128 + GPAIRS * g) * 128) * 256,
                            [[256, 128], [128 * 256, GPAIRS], [1, 256]]),
                    bass.AP(gbuf, par * 2 * GPAIRS * 256 + half * GPAIRS * 256,
                            [[2 * 2 * GPAIRS * 256, 128], [256, GPAIRS], [1, 256]]),
                    waits=[t_dve[(g, GPAIRS - 1)] if half == 0 else None],
                    inc="gst0" if par == 0 else "gst1")

        S.emit("sync", lambda e: None, waits=[S.tok("gst0"), S.tok("gst1")])

        with nc.Block() as blk:
            S.replay(blk)

    return nc


def host_prep_l2(z_by_core, inputs, core):
    c = core
    za = np.stack([z[:, 128 * c:128 * c + 128] for z in z_by_core])
    zb = np.stack([z[:, 1024 + 128 * c:1024 + 128 * c + 128] for z in z_by_core])
    za = za.reshape(8, 2, NJ, 128, 128).reshape(16, NJ, 128, 128)
    zb = zb.reshape(8, 2, NJ, 128, 128).reshape(16, NJ, 128, 128)
    zin = np.concatenate([za, zb], axis=3).reshape(16 * NJ * 128, 256)
    w_a = np.asarray(inputs["L0_attn_w_a"])[:, 0, :]
    w_b = np.asarray(inputs["L0_attn_w_b"])[:, 0, :]
    wa = w_a[128 * c:128 * c + 128]
    wb = w_b[128 * c:128 * c + 128]
    wrev = np.zeros((256, 512), np.float32)
    m = np.arange(127, 383)
    wrev[0:128, m] = wa[:, 382 - m]
    wrev[128:256, m] = wb[:, 382 - m]
    wrevcq = np.zeros((768, 256), np.float32)
    for q in range(3):
        wrevcq[q::3, :] = wrev[:, 128 * q:128 * q + 256]
    b_ab = np.concatenate([
        np.asarray(inputs["L0_attn_b_a"])[128 * c:128 * c + 128],
        np.asarray(inputs["L0_attn_b_b"])[128 * c:128 * c + 128]])[None, :]
    return {"zin": np.ascontiguousarray(zin),
            "wrevcq": wrevcq,
            "b_ab": np.ascontiguousarray(b_ab).astype(np.float32)}


L3_SEMN = ["dwo", "dbo", "dmo", "xb0", "xb1", "xs0", "xs1", "xn0", "xn1",
           "x1s0", "x1s1", "dmx", "pe", "ve", "ae", "vq"]


def build_l3():
    nc = bass.Bass("TRN2", target_bir_lowering=False, debug=False,
                   detect_race_conditions=False)
    g3 = nc.dram_tensor("g3", [E, TOK], BF16, kind="ExternalInput")
    xn = nc.dram_tensor("xn", [D, TOK], F32, kind="ExternalInput")
    w_outT = nc.dram_tensor("w_outT", [E, D], F32, kind="ExternalInput")
    b_outc = nc.dram_tensor("b_outc", [128, 16], F32, kind="ExternalInput")
    mod_in = nc.dram_tensor("mod_in", [128, 96], F32, kind="ExternalInput")
    x1o = nc.dram_tensor("x1o", [D, TOK], BF16, kind="ExternalOutput")
    max1o = nc.dram_tensor("max1o", [1, 1], F32, kind="ExternalOutput")
    mxs = nc.dram_tensor("mxs", [1, 128], F32, kind="Internal")

    ctx = contextlib.ExitStack()
    with ctx:
        sb = nc.sbuf_tensor
        wo_sb = ctx.enter_context(sb("wo_sb", [128, ECH * D], BF16))
        cat_db = ctx.enter_context(sb("cat_db", [128, 2 * ECH * 512], BF16))
        x0_db = ctx.enter_context(sb("x0_db", [128, 2 * HCH * 512], F32))
        x1_db = ctx.enter_context(sb("x1_db", [128, 2 * HCH * 512], BF16))
        xsq = ctx.enter_context(sb("xsq2", [128, HCH * 512], BF16))
        stat = ctx.enter_context(sb("stat3", [128, 8 * 512], F32))
        mod_sb = ctx.enter_context(sb("mod_sb3", [128, 96], F32))
        bo_sb = ctx.enter_context(sb("bo_sb", [128, 16], F32))
        gab_sb = ctx.enter_context(sb("gab_sb", [128, 16], F32))
        ones = ctx.enter_context(sb("ones3", [128, 128], BF16))
        eps_sb = ctx.enter_context(sb("eps3", [128, 1], F32))
        mxcol = ctx.enter_context(sb("mxcol", [128, 4], F32))
        mxrow = ctx.enter_context(sb("mxrow", [1, 128], F32))
        mx1 = ctx.enter_context(sb("mx1", [1, 1], F32))
        arena = ctx.enter_context(nc.psum_tensor("arena3", [128, 4096], F32))

        def ps(off, n):
            return bass.AP(arena, off, [[4096, 128], [1, n]])

        PS_O = 0
        PS_STAT = 1024

        sems = _sems(ctx, nc, L3_SEMN)
        S = Sched(nc, sems)
        SM, SQ, SVAR, SRSTD = (i * 512 for i in range(4))
        SSTD = SQ
        SU0 = SM
        SU1 = SU0

        def init_consts(v):
            v.memset(ones[:, :], 1.0)
            v.memset(mxcol[:, :], 0.0)
            return v.memset(eps_sb[:, :], EPS)
        t_ones = S.emit("vector", init_consts, inc="ve")

        t_wo = None
        for ec in range(ECH):
            t_wo = S.dma("gpsimd", wo_sb[:, ec * D:(ec + 1) * D],
                         bass.AP(w_outT, ec * 128 * D, [[D, 128], [1, D]]),
                         inc="dwo")
        t_bo = S.dma("sync", bo_sb[:, :], b_outc.ap(), inc="dbo")
        t_mo = S.dma("sync", mod_sb[:, :], mod_in.ap(), inc="dmo")
        t_gab = S.emit("vector",
                       lambda v: v.tensor_tensor(gab_sb[:, :], mod_sb[:, 32:48],
                                                 bo_sb[:, :], ALU.mult),
                       waits=[t_bo, t_mo], inc="ve")

        t_xbar = {}
        t_mm = {}
        t_ev = {}
        t_sq = {}
        t_mq = {}
        t_u = {}
        t_x1s = {}
        for tt in range(NTT):
            cslot = (tt % 2) * ECH * 512
            xslot = (tt % 2) * HCH * 512
            war = t_mm.get((tt - 2, HCH - 1))
            tokb = toks = None
            for ec in range(ECH):
                eng = "sync" if ec % 2 == 0 else "scalar"
                semn = ("xb0" if tt % 2 == 0 else "xb1") if eng == "sync" else \
                       ("xs0" if tt % 2 == 0 else "xs1")
                tok = S.dma(eng, bass.AP(cat_db, cslot + ec * 512,
                                         [[2 * ECH * 512, 128], [1, 512]]),
                            bass.AP(g3, (ec * 128) * TOK + tt * 512,
                                    [[TOK, 128], [1, 512]]),
                            waits=[war if ec in (0, 1) else None],
                            inc=semn)
                if eng == "sync":
                    tokb = tok
                else:
                    toks = tok
            t_xbar[tt] = (tokb, toks)
            for ch in range(HCH):
                t_x = S.dma("gpsimd",
                            x0_db[:, xslot + ch * 512: xslot + ch * 512 + 512],
                            bass.AP(xn, ch * 128 * TOK + tt * 512,
                                    [[TOK, 128], [1, 512]]),
                            waits=[t_ev.get((tt - 2, HCH - 1)) if ch == 0 else None],
                            inc="xn0" if tt % 2 == 0 else "xn1")
                if ch == HCH - 1:
                    t_xn = t_x
            for hc in range(HCH):
                po = PS_O + ((tt * HCH + hc) % 2) * 512

                def op_mm(t, cslot=cslot, hc=hc, po=po):
                    inst = None
                    for ec in range(ECH):
                        inst = t.matmul(ps(po, 512),
                                        wo_sb[:, ec * D + hc * 128:
                                              ec * D + hc * 128 + 128],
                                        cat_db[:, cslot + ec * 512:
                                               cslot + ec * 512 + 512],
                                        start=(ec == 0), stop=(ec == ECH - 1))
                    return inst
                t_mm[(tt, hc)] = S.emit(
                    "tensor", op_mm,
                    waits=[t_xbar[tt][0] if hc == 0 else None,
                           t_xbar[tt][1] if hc == 0 else None,
                           t_wo if (tt, hc) == (0, 0) else None,
                           t_ev.get((tt * HCH + hc - 2) if False else
                                    ((tt * HCH + hc - 2) // HCH,
                                     (tt * HCH + hc - 2) % HCH))],
                    inc="pe")

                def op_ev(v, tt=tt, hc=hc, po=po, xslot=xslot):
                    bq = 0 if tt < NTT // 2 else 1
                    col = hc * 2 + bq
                    v.tensor_scalar(stat[:, SU0:SU0 + 512], ps(po, 512),
                                    mod_sb[:, 32 + col:32 + col + 1],
                                    gab_sb[:, col:col + 1], ALU.mult, ALU.add)
                    return v.tensor_tensor(
                        x1_db[:, xslot + hc * 512: xslot + hc * 512 + 512],
                        stat[:, SU0:SU0 + 512],
                        x0_db[:, xslot + hc * 512: xslot + hc * 512 + 512],
                        ALU.add)
                t_ev[(tt, hc)] = S.emit(
                    "vector", op_ev,
                    waits=[t_mm[(tt, hc)], t_gab if (tt, hc) == (0, 0) else None,
                           t_xn if hc == 0 else None,
                           (t_x1s.get((tt - 2, HCH - 1)) if hc == 0 else None),
                           (t_sq.get(tt - 2) if hc == 0 else None)],
                    inc="ve")
                t_x1s[(tt, hc)] = S.dma(
                    "gpsimd", bass.AP(x1o, hc * 128 * TOK + tt * 512,
                                      [[TOK, 128], [1, 512]]),
                    x1_db[:, xslot + hc * 512: xslot + hc * 512 + 512],
                    waits=[t_ev[(tt, hc)]],
                    inc="x1s0" if tt % 2 == 0 else "x1s1")
            t_sq[tt] = S.emit("scalar",
                              lambda sc, xslot=xslot: sc.activation(
                                  out=xsq[:, :],
                                  in_=x1_db[:, xslot:xslot + HCH * 512],
                                  func=AF.Square),
                              waits=[t_ev[(tt, HCH - 1)],
                                     t_stq if tt >= 1 else None],
                              inc="ae")

            def stats_mm(t, xslot=xslot):
                inst = None
                for ch in range(HCH):
                    t.matmul(ps(PS_STAT, 512), ones[:, :],
                             x1_db[:, xslot + ch * 512: xslot + ch * 512 + 512],
                             start=(ch == 0), stop=(ch == HCH - 1))
                for ch in range(HCH):
                    inst = t.matmul(ps(PS_STAT + 512, 512), ones[:, :],
                                    xsq[:, ch * 512:(ch + 1) * 512],
                                    start=(ch == 0), stop=(ch == HCH - 1))
                return inst
            t_stq = S.emit("tensor", stats_mm,
                           waits=[t_ones if tt == 0 else None, t_sq[tt],
                                  t_mq.get(tt - 1)], inc="pe")

            def mq(v):
                v.tensor_scalar(stat[:, SM:SM + 512], ps(PS_STAT, 512),
                                1.0 / D, None, ALU.mult)
                v.tensor_scalar(stat[:, SQ:SQ + 512], ps(PS_STAT + 512, 512),
                                1.0 / D, None, ALU.mult)
                v.tensor_tensor(stat[:, SVAR:SVAR + 512], stat[:, SM:SM + 512],
                                stat[:, SM:SM + 512], ALU.mult)
                return v.tensor_tensor(stat[:, SVAR:SVAR + 512],
                                       stat[:, SQ:SQ + 512],
                                       stat[:, SVAR:SVAR + 512], ALU.subtract)
            t_mq[tt] = S.emit("vector", mq, waits=[t_stq, t_std if tt >= 1 else None],
                              inc="ve")
            t_std = S.emit("scalar",
                           lambda sc: sc.activation(out=stat[:, SSTD:SSTD + 512],
                                                    in_=stat[:, SVAR:SVAR + 512],
                                                    func=AF.Sqrt,
                                                    bias=eps_sb[:, 0:1]),
                           waits=[t_mq[tt]], inc="ae")

            def rstd_mrstd(v):
                v.reciprocal(stat[:, SRSTD:SRSTD + 512], stat[:, SSTD:SSTD + 512])
                return v.tensor_tensor(stat[:, SVAR:SVAR + 512],
                                       stat[:, SM:SM + 512],
                                       stat[:, SRSTD:SRSTD + 512], ALU.mult)
            t_rs = S.emit("vector", rstd_mrstd, waits=[t_std], inc="ve")
            bq = 0 if tt < NTT // 2 else 1
            for ch in range(HCH):
                su = SU1 if ch % 2 == 0 else SU1 + 512

                kidx = tt * HCH + ch

                def h2max(v, xslot=xslot, ch=ch, su=su, bq=bq, kidx=kidx):
                    col = ch * 2 + bq
                    if kidx >= 2:
                        v.tensor_tensor(mxcol[:, 0:1], mxcol[:, 0:1],
                                        mxcol[:, 1 + kidx % 2:2 + kidx % 2],
                                        ALU.max)
                    v.tensor_tensor(stat[:, su:su + 512],
                                    x1_db[:, xslot + ch * 512:
                                          xslot + ch * 512 + 512],
                                    stat[:, SRSTD:SRSTD + 512], ALU.mult)
                    v.tensor_tensor(stat[:, su:su + 512], stat[:, su:su + 512],
                                    stat[:, SVAR:SVAR + 512], ALU.subtract)
                    v.tensor_scalar(stat[:, su:su + 512], stat[:, su:su + 512],
                                    mod_sb[:, 64 + col:64 + col + 1],
                                    mod_sb[:, 48 + col:48 + col + 1],
                                    ALU.mult, ALU.add)
                    return v.reduce_max(mxcol[:, 1 + kidx % 2:2 + kidx % 2],
                                        stat[:, su:su + 512],
                                        axis=AX.X, apply_absolute_value=True)
                t_u[(tt, ch)] = S.emit("vector", h2max,
                                       waits=[t_rs if ch == 0 else None], inc="ve")

        def fin_acc(v):
            for kl in (1, 2):
                inst = v.tensor_tensor(mxcol[:, 0:1], mxcol[:, 0:1],
                                       mxcol[:, kl:kl + 1], ALU.max)
                inst.then_inc(sems["vq"], 1)
                S.vq_base += 1
                v.wait_ge(sems["vq"], S.vq_base)
            return v.tensor_copy(mxcol[:, 3:4], mxcol[:, 0:1])
        t_fa = S.emit("vector", fin_acc, waits=[t_u[(NTT - 1, HCH - 1)]], inc="ve")
        t_m1 = S.dma("gpsimd", bass.AP(mxs, 0, [[1, 128]]),
                     bass.AP(mxcol, 0, [[4, 128]]),
                     waits=[t_fa], inc="dmx")
        t_m2 = S.dma("gpsimd", mxrow[:, :], mxs.ap(), waits=[t_m1], inc="dmx")
        t_m3 = S.emit("vector",
                      lambda v: v.reduce_max(mx1[:, :], mxrow[:, :], axis=AX.X),
                      waits=[t_m2], inc="ve")
        t_m4 = S.dma("gpsimd", max1o.ap(), mx1[:, :], waits=[t_m3], inc="dmx")

        S.emit("gpsimd", lambda e: None,
               waits=[S.tok("x1s0"), S.tok("x1s1"), S.tok("dmx")])

        with nc.Block() as blk:
            S.replay(blk)

    return nc


def host_prep_l3(gated_by_core, inputs, core, mod_by_core):
    m = core
    g3 = np.zeros((E, BPC, NJ, 128), dtype=BFNP)
    for k in range(8):
        gk = np.asarray(gated_by_core[k]).reshape(256, 128, NJ, 16)
        for half in range(2):
            sl = gk[half * 128:half * 128 + 128]
            e0 = half * 1024 + 128 * k
            v = sl[:, :, :, 2 * m:2 * m + 2]
            g3[e0:e0 + 128] = v.transpose(0, 3, 2, 1)
    g3 = g3.reshape(E, TOK)
    b0 = m * BPC
    xs = np.asarray(inputs["L0_attn_x"][b0:b0 + BPC])
    xnr = np.ascontiguousarray(xs.transpose(2, 0, 1).reshape(D, TOK)).astype(np.float32)
    b_out = np.asarray(inputs["L0_attn_b_out"]).reshape(HCH, 128).T
    b_outc = np.repeat(b_out, 2, axis=1)
    return {"g3": np.ascontiguousarray(g3),
            "xn": xnr,
            "w_outT": np.ascontiguousarray(np.asarray(inputs["L0_attn_w_out"]).T).astype(np.float32),
            "b_outc": np.ascontiguousarray(b_outc).astype(np.float32),
            "mod_in": np.asarray(mod_by_core[m]).astype(np.float32)}


L4_SEMN = ["dx1a", "dx1b", "dmo", "dmx", "dsc", "dwr", "dm2", "dcl",
           "dws0", "dws1", "dwt0", "dwt1",
           "gs0", "gs1", "pe", "ve", "ae", "vq"]


def build_l4(stage=3):
    nc = bass.Bass("TRN2", target_bir_lowering=False, debug=False,
                   detect_race_conditions=False, dynamic_dma_scratch_size=8192)
    x1 = nc.dram_tensor("x1", [NTT * 128, HCH * 512], BF16,
                        kind="ExternalInput")
    mod_in = nc.dram_tensor("mod_in", [128, 96], F32, kind="ExternalInput")
    max1all = nc.dram_tensor("max1all", [1, 8], F32, kind="ExternalInput")
    clip_x = nc.dram_tensor("clip_x", [1, 1], F32, kind="ExternalInput")
    w_gT = nc.dram_tensor("w_gT", [D, II], F32, kind="ExternalInput")
    w_uT = nc.dram_tensor("w_uT", [D, II], F32, kind="ExternalInput")
    w_g = nc.dram_tensor("w_g", [II, D], F32, kind="ExternalInput")
    w_u = nc.dram_tensor("w_u", [II, D], F32, kind="ExternalInput")
    gout = nc.dram_tensor("gout", [II, TOK], BF16, kind="ExternalOutput")
    max2o = nc.dram_tensor("max2o", [1, 1], F32, kind="ExternalOutput")

    ctx = contextlib.ExitStack()
    with ctx:
        sb = nc.sbuf_tensor
        x1_db = ctx.enter_context(sb("x1db4", [128, 2 * HCH * 512], BF16))
        scratch8 = ctx.enter_context(sb("scr84", [128, HCH * 512], BF16))
        xsq = scratch8
        stat = ctx.enter_context(sb("stat4", [128, 4 * 512], F32))
        xq_sb = ctx.enter_context(sb("xq_sb", [128, HCH * TOK], BF16))
        codes = ctx.enter_context(sb("codes4", [128, 2 * HCH * II], BF16))
        wload = ctx.enter_context(sb("wload", [128, 2 * II], F32))
        wscales = ctx.enter_context(sb("wscales", [128, 64], F32))
        wsrec = ctx.enter_context(sb("wsrec", [128, 64], F32))
        wrowT = ctx.enter_context(sb("wrowT", [64, 128], F32))
        winv_bc = ctx.enter_context(sb("winv_bc", [128, II], BF16))
        oscl = ctx.enter_context(sb("oscl4", [128, 64], F32))
        mod_sb = ctx.enter_context(sb("mod_sb4", [128, 96], F32))
        sclq = ctx.enter_context(sb("sclq", [128, 16], F32))
        biasq = ctx.enter_context(sb("biasq", [128, 16], F32))
        sc_row = ctx.enter_context(sb("sc_row", [1, 8], F32))
        sc_one = ctx.enter_context(sb("sc_one", [1, 4], F32))
        s1bc = ctx.enter_context(sb("s1bc", [128, 2], F32))
        ones = ctx.enter_context(sb("ones4", [128, 128], BF16))
        eps_sb = ctx.enter_context(sb("eps4", [128, 1], F32))
        f16t = ctx.enter_context(sb("f16t4", [128, 512], F16))
        sg_sb = scratch8
        gtile_base = 2048
        mxcol = ctx.enter_context(sb("mxcol4", [128, 4], F32))
        mxrow = ctx.enter_context(sb("mxrow4", [1, 128], F32))
        mx1 = ctx.enter_context(sb("mx14", [1, 1], F32))
        arena = ctx.enter_context(nc.psum_tensor("arena4", [128, 4096], F32))
        scd = nc.dram_tensor("scd4", [1, 4], F32, kind="Internal")
        wrd = nc.dram_tensor("wrd4", [64, 128], F32, kind="Internal")
        mxs = nc.dram_tensor("mxs4", [1, 128], F32, kind="Internal")

        def ps(off, n):
            return bass.AP(arena, off, [[4096, 128], [1, n]])

        PS_G = 0
        PS_U = 2048
        PS_STAT = 0
        PS_B = 2048

        sems = _sems(ctx, nc, L4_SEMN)
        S = Sched(nc, sems)
        SM, SQ, SVAR, SRSTD = (i * 512 for i in range(4))
        SSTD = SQ
        SU0 = SM
        SU1 = SU0

        def init_consts(v):
            v.memset(ones[:, :], 1.0)
            v.memset(mxcol[:, :], 0.0)
            v.memset(wscales[:, :], 1.0)
            v.memset(wsrec[:, :], 1.0)
            return v.memset(eps_sb[:, :], EPS)
        t_ones = S.emit("vector", init_consts, inc="ve")

        t_mo = S.dma("sync", mod_sb[:, :], mod_in.ap(), inc="dmo")
        t_mxa = S.dma("sync", sc_row[:, :], max1all.ap(), inc="dmx")
        t_cl = S.dma("sync", sc_one[:, 3:4], clip_x.ap(), inc="dcl")

        def s1chain_v(v):
            return v.reduce_max(sc_one[:, 0:1], sc_row[:, :], axis=AX.X)
        t_gm = S.emit("vector", s1chain_v, waits=[t_mxa], inc="ve")
        t_sg = S.emit("scalar",
                      lambda sc: sc.activation(out=sc_one[:, 2:3], in_=sc_one[:, 3:4],
                                               func=AF.Sigmoid),
                      waits=[t_cl], inc="ae")

        def s1fin(v):
            cnt = [0]

            def stp(inst):
                cnt[0] += 1
                inst.then_inc(sems["vq"], 1)
                v.wait_ge(sems["vq"], S.vq_base + cnt[0])
            stp(v.tensor_scalar(sc_one[:, 0:1], sc_one[:, 0:1], sc_one[:, 2:3],
                                1.0 / QMAX, ALU.mult, ALU.mult))
            stp(v.reciprocal(sc_one[:, 1:2], sc_one[:, 0:1]))
            S.vq_base += cnt[0]
            return v.tensor_copy(sc_one[:, 2:3], sc_one[:, 1:2])
        t_s1 = S.emit("vector", s1fin, waits=[t_gm, t_sg], inc="ve")
        t_sd = S.dma("gpsimd", scd.ap(), sc_one[:, 0:4], waits=[t_s1], inc="dsc")
        t_s1b = S.dma("gpsimd", s1bc[:, :],
                      bass.AP(scd, 0, [[0, 128], [1, 2]]), waits=[t_sd], inc="dsc")

        def qscale(v):
            v.tensor_scalar(sclq[:, :], mod_sb[:, 64:80], s1bc[:, 1:2], None,
                            ALU.mult)
            return v.tensor_scalar(biasq[:, :], mod_sb[:, 48:64], s1bc[:, 1:2],
                                   None, ALU.mult)
        t_qs = S.emit("vector", qscale, waits=[t_s1b, t_mo], inc="ve")

        t_ws = {}
        t_wsc = {}
        for w in range(2):
            wnat = w_g if w == 0 else w_u
            for icp in range(ICH // 2):
                k = w * (ICH // 2) + icp
                par = k % 2
                t_ld = S.dma("sync", wload[:, par * II:par * II + 2 * D],
                             bass.AP(wnat, 2 * icp * 128 * D,
                                     [[D, 128], [128 * D, 2], [1, D]]),
                             waits=[t_wsc.get(k - 2)],
                             inc="dws0" if par == 0 else "dws1")

                def wsc_fn(v, w=w, icp=icp, par=par):
                    col = w * ICH + 2 * icp
                    v.reduce_max(wscales[:, col:col + 1],
                                 wload[:, par * II:par * II + D], axis=AX.X,
                                 apply_absolute_value=True)
                    return v.reduce_max(wscales[:, col + 1:col + 2],
                                        wload[:, par * II + D:par * II + 2 * D],
                                        axis=AX.X, apply_absolute_value=True)
                t_wsc[k] = S.emit("vector", wsc_fn, waits=[t_ld], inc="ve")
        def ws_tail(v):
            cnt = [0]

            def stp(inst):
                cnt[0] += 1
                inst.then_inc(sems["vq"], 1)
                v.wait_ge(sems["vq"], S.vq_base + cnt[0])
            stp(v.tensor_scalar(oscl[:, :], wscales[:, :], s1bc[:, 0:1],
                                1.0 / QMAX, ALU.mult, ALU.mult))
            stp(v.reciprocal(wsrec[:, :], wscales[:, :]))
            stp(v.tensor_single_scalar(wsrec[:, :], wsrec[:, :], QMAX, ALU.mult))
            S.vq_base += cnt[0]
            return v.tensor_copy(mxcol[:, 3:4], wsrec[:, 0:1])
        t_oscl = S.emit("vector", ws_tail,
                        waits=[t_wsc[ICH - 1], t_s1b], inc="ve")
        def wtr(v):
            inst = None
            for pb in range(4):
                for cb in range(2):
                    inst = v.transpose(wrowT[cb * 32:(cb + 1) * 32,
                                             pb * 32:(pb + 1) * 32],
                                       wsrec[pb * 32:(pb + 1) * 32,
                                             cb * 32:(cb + 1) * 32])
            return inst
        t_wtr = S.emit("vector", wtr, waits=[t_wsc[ICH - 1]], inc="ve")

        t_code = {}
        t_bc = {}
        t_wrd = S.dma("gpsimd", bass.AP(wrd, 0, [[128, 64], [1, 128]]),
                      bass.AP(wrowT, 0, [[128, 64], [1, 128]]),
                      waits=[t_wtr], inc="dwr")
        t_bc[0] = S.dma(
            "gpsimd", winv_bc[:, :],
            bass.AP(wrd, 0, [[0, 128], [128, ICH], [1, 128]]),
            waits=[t_wrd], inc="dwr")
        t_wt = {}

        def emit_code(cidx):
            w, ch = divmod(cidx, HCH)
            wT = w_gT if w == 0 else w_uT
            if cidx == HCH:
                t_bc[1] = S.dma(
                    "gpsimd", winv_bc[:, :],
                    bass.AP(wrd, ICH * 128, [[0, 128], [128, ICH], [1, 128]]),
                    waits=[t_wrd, t_code[cidx - 1]], inc="dwr")
            par = cidx % 2
            t_wt[cidx] = S.dma(
                "sync", wload[:, par * II:(par + 1) * II],
                bass.AP(wT, ch * 128 * II, [[II, 128], [1, II]]),
                waits=[t_code.get(cidx - 2),
                       t_wsc[ICH - 1] if cidx == 0 else None],
                inc="dwt0" if par == 0 else "dwt1")

            def code_fn(v, w=w, ch=ch, par=par):
                base = (w * HCH + ch) * II
                dst = codes[:, base:base + II]
                v.tensor_tensor(dst, wload[:, par * II:(par + 1) * II],
                                winv_bc[:, :], ALU.mult)
                inst = None
                for off in range(0, II, 512):
                    wd = min(512, II - off)
                    v.tensor_single_scalar(f16t[:, 0:wd],
                                           dst[:, off:off + wd], MAGIC,
                                           ALU.add)
                    inst = v.tensor_single_scalar(dst[:, off:off + wd],
                                                  f16t[:, 0:wd], MAGIC,
                                                  ALU.subtract)
                return inst
            t_code[cidx] = S.emit(
                "vector", code_fn,
                waits=[t_wt[cidx], t_bc[w] if ch == 0 else None],
                inc="ve")

        if stage == 0:
            S.emit("gpsimd", lambda e: None, waits=[S.tok("dwr")])
            with nc.Block() as blk:
                S.replay(blk)
            return nc
        t_x1l = {}
        t_xq = {}
        t_sq = {}
        t_mq = {}
        t_stq = None
        t_std = None
        for tt in range(NTT):
            xslot = (tt % 2) * HCH * 512
            t_x1l[tt] = S.dma(
                "gpsimd", x1_db[:, xslot:xslot + HCH * 512],
                bass.AP(x1, tt * 128 * HCH * 512,
                        [[HCH * 512, 128], [1, HCH * 512]]),
                waits=[t_xq.get((tt - 2, HCH - 1))],
                inc="dx1a" if tt % 2 == 0 else "dx1b")
            if stage == 9:
                continue
            t_sq[tt] = S.emit("scalar",
                              lambda sc, xslot=xslot: sc.activation(
                                  out=xsq[:, :],
                                  in_=x1_db[:, xslot:xslot + HCH * 512],
                                  func=AF.Square),
                              waits=[t_x1l[tt], t_stq if tt >= 1 else None],
                              inc="ae")
            if stage == 8:
                continue

            pstt = PS_STAT + (tt % 2) * 1024

            def stats_mm(t, xslot=xslot, pstt=pstt):
                inst = None
                for ch in range(HCH):
                    t.matmul(ps(pstt, 512), ones[:, :],
                             x1_db[:, xslot + ch * 512: xslot + ch * 512 + 512],
                             start=(ch == 0), stop=(ch == HCH - 1))
                for ch in range(HCH):
                    inst = t.matmul(ps(pstt + 512, 512), ones[:, :],
                                    xsq[:, ch * 512:(ch + 1) * 512],
                                    start=(ch == 0), stop=(ch == HCH - 1))
                return inst
            t_stq = S.emit("tensor", stats_mm,
                           waits=[t_ones if tt == 0 else None, t_sq[tt],
                                  t_mq.get(tt - 2)],
                           inc="pe")

            def mq(v, pstt=pstt):
                v.tensor_scalar(stat[:, SM:SM + 512], ps(pstt, 512),
                                1.0 / D, None, ALU.mult)
                v.tensor_scalar(stat[:, SQ:SQ + 512], ps(pstt + 512, 512),
                                1.0 / D, None, ALU.mult)
                v.tensor_tensor(stat[:, SVAR:SVAR + 512], stat[:, SM:SM + 512],
                                stat[:, SM:SM + 512], ALU.mult)
                return v.tensor_tensor(stat[:, SVAR:SVAR + 512],
                                       stat[:, SQ:SQ + 512],
                                       stat[:, SVAR:SVAR + 512], ALU.subtract)
            if stage == 10:
                continue
            t_mq[tt] = S.emit("vector", mq,
                              waits=[t_stq, t_std if tt >= 1 else None], inc="ve")
            t_std = S.emit("scalar",
                           lambda sc: sc.activation(out=stat[:, SSTD:SSTD + 512],
                                                    in_=stat[:, SVAR:SVAR + 512],
                                                    func=AF.Sqrt,
                                                    bias=eps_sb[:, 0:1]),
                           waits=[t_mq[tt]], inc="ae")

            def rstd_mrstd(v):
                v.reciprocal(stat[:, SRSTD:SRSTD + 512], stat[:, SSTD:SSTD + 512])
                return v.tensor_tensor(stat[:, SVAR:SVAR + 512],
                                       stat[:, SM:SM + 512],
                                       stat[:, SRSTD:SRSTD + 512], ALU.mult)
            t_rs = S.emit("vector", rstd_mrstd, waits=[t_std], inc="ve")
            bq = 0 if tt < NTT // 2 else 1
            for ch in range(HCH):
                su = SU0 if ch % 2 == 0 else SU1

                def xq_fn(v, xslot=xslot, tt=tt, ch=ch, su=su, bq=bq):
                    col = ch * 2 + bq
                    dst = xq_sb[:, ch * TOK + tt * 512: ch * TOK + tt * 512 + 512]
                    v.tensor_tensor(stat[:, su:su + 512],
                                    x1_db[:, xslot + ch * 512:
                                          xslot + ch * 512 + 512],
                                    stat[:, SRSTD:SRSTD + 512], ALU.mult)
                    v.tensor_tensor(stat[:, su:su + 512], stat[:, su:su + 512],
                                    stat[:, SVAR:SVAR + 512], ALU.subtract)
                    v.tensor_scalar(stat[:, su:su + 512], stat[:, su:su + 512],
                                    sclq[:, col:col + 1],
                                    biasq[:, col:col + 1],
                                    ALU.mult, ALU.add)
                    v.tensor_scalar(stat[:, su:su + 512], stat[:, su:su + 512],
                                    CLIP_LO, CLIP_HI, ALU.max, ALU.min)
                    v.tensor_single_scalar(f16t[:, 0:512], stat[:, su:su + 512],
                                           MAGIC, ALU.add)
                    return v.tensor_single_scalar(dst, f16t[:, 0:512], MAGIC,
                                                  ALU.subtract)
                if stage != 11:
                    t_xq[(tt, ch)] = S.emit(
                        "vector", xq_fn,
                        waits=[t_rs if ch == 0 else None,
                               t_qs if (tt, ch) == (0, 0) else None],
                        inc="ve")
            if stage not in (8, 9, 10, 11):
                emit_code(2 * tt)
                emit_code(2 * tt + 1)

        if stage in (1, 8, 9, 10, 11):
            S.emit("gpsimd", lambda e: None, waits=[S.tok("dx1a"), S.tok("dx1b")])
            with nc.Block() as blk:
                S.replay(blk)
            return nc
        t_gst = {}
        t_gd = {}
        slot = 0
        lastq = t_xq.get((NTT - 1, HCH - 1))
        for th in range(2 if stage == 2 else 4):
            for ic in range(ICH):
                spar = slot % 2
                PSG = spar * 1024
                PSU = 2048 + spar * 1024

                def gu_mm(t, th=th, ic=ic, PSG=PSG, PSU=PSU):
                    inst = None
                    for w in range(2):
                        pbase = PSG if w == 0 else PSU
                        for ch in range(HCH):
                            lhs = codes[:, (w * HCH + ch) * II + ic * 128:
                                        (w * HCH + ch) * II + (ic + 1) * 128]
                            for q in range(2):
                                tt = th * 2 + q
                                mvx = xq_sb[:, ch * TOK + tt * 512:
                                            ch * TOK + tt * 512 + 512]
                                inst = t.matmul(ps(pbase + q * 512, 512), lhs, mvx,
                                                start=(ch == 0),
                                                stop=(ch == HCH - 1))
                    return inst
                t_mmgu = S.emit(
                    "tensor", gu_mm,
                    waits=[lastq if slot == 0 else None,
                           t_code[2 * HCH - 1] if slot == 0 else None,
                           t_gd.get((slot - 2, 1))], inc="pe")

                def gseq_act(sc, ic=ic, PSG=PSG, spar=spar):
                    inst = None
                    for q in range(2):
                        inst = sc.activation(
                            out=sg_sb[:, (spar * 2 + q) * 512:
                                      (spar * 2 + q) * 512 + 512],
                            in_=ps(PSG + q * 512, 512), func=AF.Silu,
                            scale=oscl[:, ic:ic + 1])
                    return inst
                t_ga = S.emit("scalar", gseq_act, waits=[t_mmgu], inc="ae")

                for q in range(2):
                    kidx = slot * 2 + q

                    def gq_dve(v, ic=ic, q=q, spar=spar, PSU=PSU, kidx=kidx):
                        if kidx >= 2:
                            v.tensor_tensor(mxcol[:, 0:1], mxcol[:, 0:1],
                                            mxcol[:, 1 + kidx % 2:2 + kidx % 2],
                                            ALU.max)
                        sg = sg_sb[:, (spar * 2 + q) * 512:
                                   (spar * 2 + q) * 512 + 512]
                        gt = scratch8[:, gtile_base + (kidx % 3) * 512:
                                      gtile_base + (kidx % 3) * 512 + 512]
                        v.tensor_scalar(gt, ps(PSU + q * 512, 512),
                                        oscl[:, ICH + ic:ICH + ic + 1], None,
                                        ALU.mult)
                        v.tensor_tensor(gt, gt, sg, ALU.mult)
                        return v.reduce_max(mxcol[:, 1 + kidx % 2:2 + kidx % 2],
                                            gt, axis=AX.X,
                                            apply_absolute_value=True)
                    t_gd[(slot, q)] = S.emit(
                        "vector", gq_dve,
                        waits=[t_ga if q == 0 else None,
                               t_gst.get((slot - 2, 1) if q == 0 else
                                         (slot - 1, 0))],
                        inc="ve")
                    tt = th * 2 + q
                    t_gst[(slot, q)] = S.dma(
                        "sync", bass.AP(gout, ic * 128 * TOK + tt * 512,
                                        [[TOK, 128], [1, 512]]),
                        scratch8[:, gtile_base + (kidx % 3) * 512:
                                 gtile_base + (kidx % 3) * 512 + 512],
                        waits=[t_gd[(slot, q)]],
                        inc="gs0" if slot % 2 == 0 else "gs1")
                slot += 1

        def fin_acc4(v):
            for kl in (1, 2):
                inst = v.tensor_tensor(mxcol[:, 0:1], mxcol[:, 0:1],
                                       mxcol[:, kl:kl + 1], ALU.max)
                inst.then_inc(sems["vq"], 1)
                S.vq_base += 1
                v.wait_ge(sems["vq"], S.vq_base)
            return v.tensor_copy(mxcol[:, 3:4], mxcol[:, 0:1])
        t_fa4 = S.emit("vector", fin_acc4, waits=[t_gd[(slot - 1, 1)]], inc="ve")
        t_m1 = S.dma("gpsimd", bass.AP(mxs, 0, [[1, 128]]),
                     bass.AP(mxcol, 0, [[4, 128]]),
                     waits=[t_fa4], inc="dm2")
        t_m2 = S.dma("gpsimd", mxrow[:, :], mxs.ap(), waits=[t_m1], inc="dm2")
        t_m3 = S.emit("vector",
                      lambda v: v.reduce_max(mx1[:, :], mxrow[:, :], axis=AX.X),
                      waits=[t_m2], inc="ve")
        t_m4 = S.dma("gpsimd", max2o.ap(), mx1[:, :], waits=[t_m3], inc="dm2")
        S.emit("gpsimd", lambda e: None,
               waits=[S.tok("gs0"), S.tok("gs1"), S.tok("dm2")])

        with nc.Block() as blk:
            S.replay(blk)

    return nc


def host_prep_l4(x1_by_core, inputs, core, mod_by_core, max1_by_core):
    m = core
    x1c = np.asarray(x1_by_core[m])
    x1t = np.ascontiguousarray(
        x1c.reshape(HCH, 128, NTT, 512).transpose(2, 1, 0, 3)
        .reshape(NTT * 128, HCH * 512))
    return {
        "x1": x1t,
        "mod_in": np.asarray(mod_by_core[m]).astype(np.float32),
        "max1all": np.concatenate([np.asarray(v).reshape(1, 1)
                                   for v in max1_by_core], axis=1).astype(np.float32),
        "clip_x": np.asarray(inputs["L0_ffn_clip_x"]).reshape(1, 1).astype(np.float32),
        "w_gT": np.ascontiguousarray(np.asarray(inputs["L0_ffn_w_gate"]).T).astype(np.float32),
        "w_uT": np.ascontiguousarray(np.asarray(inputs["L0_ffn_w_up"]).T).astype(np.float32),
        "w_g": np.ascontiguousarray(np.asarray(inputs["L0_ffn_w_gate"])).astype(np.float32),
        "w_u": np.ascontiguousarray(np.asarray(inputs["L0_ffn_w_up"])).astype(np.float32),
    }


L5_SEMN = ["dx1", "dgl", "dmo", "dmx", "dsc", "dwr", "dcl", "dws0", "dws1",
           "dwt0", "dwt1", "dos", "pe", "ve", "ae", "vq"]


def build_l5():
    nc = bass.Bass("TRN2", target_bir_lowering=False, debug=False,
                   detect_race_conditions=False)
    g = nc.dram_tensor("g", [II, TOK], BF16, kind="ExternalInput")
    x1 = nc.dram_tensor("x1", [D, TOK], BF16, kind="ExternalInput")
    mod_in = nc.dram_tensor("mod_in", [128, 96], F32, kind="ExternalInput")
    max2all = nc.dram_tensor("max2all", [1, 8], F32, kind="ExternalInput")
    clip_d = nc.dram_tensor("clip_d", [1, 1], F32, kind="ExternalInput")
    w_d = nc.dram_tensor("w_d", [D, II], F32, kind="ExternalInput")
    w_dT = nc.dram_tensor("w_dT", [II, D], F32, kind="ExternalInput")
    outt = nc.dram_tensor("outt", [D, TOK], F32, kind="ExternalOutput")
    scd = nc.dram_tensor("scd5", [1, 4], F32, kind="Internal")
    wrd = nc.dram_tensor("wrd5", [32, 128], F32, kind="Internal")

    ctx = contextlib.ExitStack()
    with ctx:
        sb = nc.sbuf_tensor
        codes_d = ctx.enter_context(sb("codes_d", [128, ICH * D], BF16))
        gsrc = ctx.enter_context(sb("gsrc", [128, 2 * ICH * 512], BF16))
        x1t = ctx.enter_context(sb("x1t5", [128, HCH * 512], BF16))
        out_sb = ctx.enter_context(sb("out_sb", [128, HCH * 512], F32))
        wt_db = ctx.enter_context(sb("wt_db5", [128, 2 * D], F32))
        ws_db = ctx.enter_context(sb("ws_db5", [128, 2 * II], F32))
        dsc_raw = ctx.enter_context(sb("dsc_raw", [128, 32], F32))
        drec = ctx.enter_context(sb("drec", [128, 32], F32))
        dscl16 = ctx.enter_context(sb("dscl16", [128, 16], F32))
        osclgb = ctx.enter_context(sb("osclgb", [128, 16], F32))
        wrowT = ctx.enter_context(sb("wrowT5", [32, 128], F32))
        winv_bcd = ctx.enter_context(sb("winv_bcd", [128, D], BF16))
        mod_sb = ctx.enter_context(sb("mod_sb5", [128, 96], F32))
        sc_row = ctx.enter_context(sb("sc_row5", [1, 8], F32))
        sc_one = ctx.enter_context(sb("sc_one5", [1, 4], F32))
        s2bc = ctx.enter_context(sb("s2bc", [128, 2], F32))
        tmpq = ctx.enter_context(sb("tmpq", [128, 512], F32))
        f16q = ctx.enter_context(sb("f16q", [128, 512], F16))
        arena = ctx.enter_context(nc.psum_tensor("arena5", [128, 4096], F32))

        def ps(off, n):
            return bass.AP(arena, off, [[4096, 128], [1, n]])

        sems = _sems(ctx, nc, L5_SEMN)
        S = Sched(nc, sems)

        def init_consts(v):
            v.memset(dsc_raw[:, :], 1.0)
            return v.memset(drec[:, :], 1.0)
        t_init = S.emit("vector", init_consts, inc="ve")

        t_mo = S.dma("sync", mod_sb[:, :], mod_in.ap(), inc="dmo")
        t_mxa = S.dma("sync", sc_row[:, :], max2all.ap(), inc="dmx")
        t_cl = S.dma("sync", sc_one[:, 3:4], clip_d.ap(), inc="dcl")

        t_gm = S.emit("vector",
                      lambda v: v.reduce_max(sc_one[:, 0:1], sc_row[:, :],
                                             axis=AX.X),
                      waits=[t_mxa], inc="ve")
        t_sg = S.emit("scalar",
                      lambda sc: sc.activation(out=sc_one[:, 2:3],
                                               in_=sc_one[:, 3:4],
                                               func=AF.Sigmoid),
                      waits=[t_cl], inc="ae")

        def s2fin(v):
            cnt = [0]

            def stp(inst):
                cnt[0] += 1
                inst.then_inc(sems["vq"], 1)
                v.wait_ge(sems["vq"], S.vq_base + cnt[0])
            stp(v.tensor_scalar(sc_one[:, 0:1], sc_one[:, 0:1], sc_one[:, 2:3],
                                1.0 / QMAX, ALU.mult, ALU.mult))
            stp(v.reciprocal(sc_one[:, 1:2], sc_one[:, 0:1]))
            S.vq_base += cnt[0]
            return v.tensor_copy(sc_one[:, 2:3], sc_one[:, 1:2])
        t_s2 = S.emit("vector", s2fin, waits=[t_gm, t_sg], inc="ve")
        t_sd = S.dma("gpsimd", scd.ap(), sc_one[:, 0:4], waits=[t_s2], inc="dsc")
        t_s2b = S.dma("gpsimd", s2bc[:, :],
                      bass.AP(scd, 0, [[0, 128], [1, 2]]), waits=[t_sd], inc="dsc")

        t_dsc = {}
        for hc in range(HCH):
            par = hc % 2
            t_ld = S.dma("sync", ws_db[:, par * II:(par + 1) * II],
                         bass.AP(w_d, hc * 128 * II, [[II, 128], [1, II]]),
                         waits=[t_dsc.get(hc - 2)],
                         inc="dws0" if par == 0 else "dws1")
            t_dsc[hc] = S.emit(
                "vector",
                lambda v, hc=hc, par=par: v.reduce_max(
                    dsc_raw[:, hc:hc + 1], ws_db[:, par * II:(par + 1) * II],
                    axis=AX.X, apply_absolute_value=True),
                waits=[t_ld], inc="ve")

        def dtail(v):
            cnt = [0]

            def stp(inst):
                cnt[0] += 1
                inst.then_inc(sems["vq"], 1)
                v.wait_ge(sems["vq"], S.vq_base + cnt[0])
            stp(v.tensor_scalar(drec[:, 16:24], dsc_raw[:, 0:8], s2bc[:, 0:1],
                                1.0 / QMAX, ALU.mult, ALU.mult))
            stp(v.tensor_copy(
                bass.AP(dscl16, 0, [[16, 128], [2, 8]]), drec[:, 16:24]))
            stp(v.tensor_copy(
                bass.AP(dscl16, 1, [[16, 128], [2, 8]]), drec[:, 16:24]))
            stp(v.tensor_tensor(osclgb[:, :], dscl16[:, :], mod_sb[:, 80:96],
                                ALU.mult))
            stp(v.reciprocal(drec[:, 0:8], dsc_raw[:, 0:8]))
            stp(v.tensor_single_scalar(drec[:, 0:8], drec[:, 0:8], QMAX,
                                       ALU.mult))
            S.vq_base += cnt[0]
            inst = None
            for pb in range(4):
                inst = v.transpose(wrowT[0:32, pb * 32:(pb + 1) * 32],
                                   drec[pb * 32:(pb + 1) * 32, 0:32])
            return inst
        t_dt = S.emit("vector", dtail,
                      waits=[t_dsc[HCH - 1], t_s2b, t_mo], inc="ve")
        t_wrd = S.dma("gpsimd", bass.AP(wrd, 0, [[128, 32], [1, 128]]),
                      bass.AP(wrowT, 0, [[128, 32], [1, 128]]),
                      waits=[t_dt], inc="dwr")
        t_bcd = S.dma("gpsimd", winv_bcd[:, :],
                      bass.AP(wrd, 0, [[0, 128], [128, HCH], [1, 128]]),
                      waits=[t_wrd], inc="dwr")

        t_code = {}
        for ic in range(ICH):
            par = ic % 2
            t_wt = S.dma("sync", wt_db[:, par * D:(par + 1) * D],
                         bass.AP(w_dT, ic * 128 * D, [[D, 128], [1, D]]),
                         waits=[t_code.get(ic - 2)],
                         inc="dwt0" if par == 0 else "dwt1")

            def code_fn(v, ic=ic, par=par):
                dst = codes_d[:, ic * D:(ic + 1) * D]
                v.tensor_tensor(dst, wt_db[:, par * D:(par + 1) * D],
                                winv_bcd[:, :], ALU.mult)
                v.tensor_single_scalar(f16q[:, :], dst[:, 0:512], MAGIC, ALU.add)
                v.tensor_single_scalar(dst[:, 0:512], f16q[:, :], MAGIC,
                                       ALU.subtract)
                v.tensor_single_scalar(f16q[:, :], dst[:, 512:1024], MAGIC,
                                       ALU.add)
                return v.tensor_single_scalar(dst[:, 512:1024], f16q[:, :],
                                              MAGIC, ALU.subtract)
            t_code[ic] = S.emit("vector", code_fn,
                                waits=[t_wt, t_bcd if ic == 0 else None],
                                inc="ve")

        t_gl = {}
        t_hq = {}
        t_mm = {}
        t_ev = {}
        t_os = {}

        def emit_gl(tt):
            gslot = (tt % 2) * ICH * 512
            t_gl[tt] = S.dma(
                "sync", bass.AP(gsrc, gslot,
                                [[2 * ICH * 512, 128], [512, ICH], [1, 512]]),
                bass.AP(g, tt * 512, [[TOK, 128], [128 * TOK, ICH], [1, 512]]),
                waits=[t_mm.get((tt - 2, HCH - 1))], inc="dgl")

        def emit_hq(tt, ic):
            gslot = (tt % 2) * ICH * 512

            def hq_fn(v, ic=ic, gslot=gslot):
                sl = gsrc[:, gslot + ic * 512:gslot + (ic + 1) * 512]
                v.tensor_scalar(tmpq[:, :], sl, s2bc[:, 1:2], CLIP_LO,
                                ALU.mult, ALU.max)
                v.tensor_single_scalar(tmpq[:, :], tmpq[:, :], CLIP_HI, ALU.min)
                v.tensor_single_scalar(f16q[:, :], tmpq[:, :], MAGIC, ALU.add)
                return v.tensor_single_scalar(sl, f16q[:, :], MAGIC,
                                              ALU.subtract)
            t_hq[(tt, ic)] = S.emit(
                "vector", hq_fn,
                waits=[t_gl[tt] if ic == 0 else None,
                       (t_s2b if (tt, ic) == (0, 0) else None)],
                inc="ve")

        for tt in range(NTT):
            gslot = (tt % 2) * ICH * 512
            if tt == 0:
                emit_gl(0)
            if tt + 1 < NTT:
                emit_gl(tt + 1)
            t_x1 = S.dma(
                "sync", bass.AP(x1t, 0, [[HCH * 512, 128], [512, HCH], [1, 512]]),
                bass.AP(x1, tt * 512, [[TOK, 128], [128 * TOK, HCH], [1, 512]]),
                waits=[t_ev.get((tt - 1, HCH - 1))], inc="dx1")
            if tt == 0:
                for ic in range(ICH):
                    emit_hq(0, ic)
            for hc in range(HCH):
                po = ((tt * HCH + hc) % 2) * 512

                def dmm(t, hc=hc, po=po, gslot=gslot):
                    inst = None
                    for ic in range(ICH):
                        inst = t.matmul(ps(po, 512),
                                        codes_d[:, ic * D + hc * 128:
                                                ic * D + hc * 128 + 128],
                                        gsrc[:, gslot + ic * 512:
                                             gslot + (ic + 1) * 512],
                                        start=(ic == 0), stop=(ic == ICH - 1))
                    return inst
                t_mm[(tt, hc)] = S.emit(
                    "tensor", dmm,
                    waits=[t_hq[(tt, ICH - 1)] if hc == 0 else None,
                           t_code[ICH - 1] if (tt, hc) == (0, 0) else None,
                           t_ev.get(((tt * HCH + hc - 2) // HCH,
                                     (tt * HCH + hc - 2) % HCH))],
                    inc="pe")

                def dev_fn(v, tt=tt, hc=hc, po=po):
                    bq = 0 if tt < NTT // 2 else 1
                    col = hc * 2 + bq
                    v.tensor_scalar(tmpq[:, :], ps(po, 512),
                                    osclgb[:, col:col + 1], None, ALU.mult)
                    return v.tensor_tensor(
                        out_sb[:, hc * 512:(hc + 1) * 512], tmpq[:, :],
                        x1t[:, hc * 512:(hc + 1) * 512], ALU.add)
                t_ev[(tt, hc)] = S.emit(
                    "vector", dev_fn,
                    waits=[t_mm[(tt, hc)], t_x1 if hc == 0 else None,
                           t_dt if (tt, hc) == (0, 0) else None,
                           (t_os.get(tt - 1) if hc == 0 else None)],
                    inc="ve")
                if tt + 1 < NTT:
                    for ic in range(3 * hc, min(ICH, 3 * hc + 3)):
                        emit_hq(tt + 1, ic)
            t_os[tt] = S.dma(
                "gpsimd", bass.AP(outt, tt * 512,
                                  [[TOK, 128], [128 * TOK, HCH], [1, 512]]),
                bass.AP(out_sb, 0, [[HCH * 512, 128], [512, HCH], [1, 512]]),
                waits=[t_ev[(tt, HCH - 1)]], inc="dos")

        S.emit("gpsimd", lambda e: None, waits=[S.tok("dos")])

        with nc.Block() as blk:
            S.replay(blk)

    return nc


def host_prep_l5(g_by_core, x1_by_core, inputs, core, mod_by_core, max2_by_core):
    m = core
    return {
        "g": np.asarray(g_by_core[m]),
        "x1": np.asarray(x1_by_core[m]),
        "mod_in": np.asarray(mod_by_core[m]).astype(np.float32),
        "max2all": np.concatenate([np.asarray(v).reshape(1, 1)
                                   for v in max2_by_core], axis=1).astype(np.float32),
        "clip_d": np.asarray(inputs["L0_ffn_clip_down"]).reshape(1, 1).astype(np.float32),
        "w_d": np.ascontiguousarray(np.asarray(inputs["L0_ffn_w_down"])).astype(np.float32),
        "w_dT": np.ascontiguousarray(np.asarray(inputs["L0_ffn_w_down"]).T).astype(np.float32),
    }


def host_prep_l1(inputs, core):
    b0 = core * BPC
    xs = np.asarray(inputs["L0_attn_x"][b0:b0 + BPC])
    xr = xs.reshape(BPC, NJ, 128, D)[:, :, ::-1, :].reshape(BPC, T, D)
    xrev = xr.transpose(2, 0, 1).reshape(D, TOK)
    xrev = np.ascontiguousarray(
        xrev.reshape(HCH, 128, NTT, 512).transpose(2, 1, 0, 3)
        .reshape(NTT * 128, HCH * 512))
    cts = np.ascontiguousarray(np.asarray(inputs["L0_c"][b0:b0 + BPC]).T)
    return {
        "xrev": xrev.astype(np.float32),
        "cts": cts.astype(np.float32),
        "wmod": np.ascontiguousarray(inputs["L0_wmod"]).astype(np.float32),
        "w_inT": np.ascontiguousarray(np.asarray(inputs["L0_attn_w_in"]).T).astype(np.float32),
        "b_in": np.ascontiguousarray(np.asarray(inputs["L0_attn_b_in"])[None, :]).astype(np.float32),
    }


def kernel(**inputs):
    cores = list(range(NCORES))
    r1 = run_bass_kernel_spmd(build_l1(), [host_prep_l1(inputs, c) for c in cores],
                              core_ids=cores)
    z_by = [np.asarray(r1.results[c]["zout"]) for c in cores]
    mod_by = [np.asarray(r1.results[c]["mod_out"]) for c in cores]

    r2 = run_bass_kernel_spmd(build_l2(), [host_prep_l2(z_by, inputs, c)
                                           for c in cores], core_ids=cores)
    gated_by = [np.asarray(r2.results[c]["gated"]) for c in cores]

    r3 = run_bass_kernel_spmd(build_l3(), [host_prep_l3(gated_by, inputs, c, mod_by)
                                           for c in cores], core_ids=cores)
    x1_by = [np.asarray(r3.results[c]["x1o"]) for c in cores]
    max1_by = [np.asarray(r3.results[c]["max1o"]) for c in cores]

    r4 = run_bass_kernel_spmd(build_l4(), [host_prep_l4(x1_by, inputs, c, mod_by,
                                                        max1_by) for c in cores],
                              core_ids=cores)
    g_by = [np.asarray(r4.results[c]["gout"]) for c in cores]
    max2_by = [np.asarray(r4.results[c]["max2o"]) for c in cores]

    r5 = run_bass_kernel_spmd(build_l5(), [host_prep_l5(g_by, x1_by, inputs, c,
                                                        mod_by, max2_by)
                                           for c in cores], core_ids=cores)
    out = np.empty((NCORES * BPC, T, D), np.float32)
    for c in cores:
        o = np.asarray(r5.results[c]["outt"])
        out[2 * c] = o[:, :T].T
        out[2 * c + 1] = o[:, T:].T
    return out
